# version 08
# baseline (speedup 1.0000x reference)
from contextlib import ExitStack

import numpy as np
import ml_dtypes

import concourse.bass as bass
import concourse.mybir as mybir
import concourse.library_config as library_config
from concourse.library_overlay import lower_extended_insts
from concourse.bass_utils import run_bass_kernel_spmd

F32 = mybir.dt.float32
BF16 = mybir.dt.bfloat16
I32 = mybir.dt.int32
I16 = mybir.dt.int16
U32 = mybir.dt.uint32
AL = mybir.AluOpType
AF = mybir.ActivationFunctionType
AX = mybir.AxisListType

NCORE = 8
TTOK = 16384
TOKC = 2048
D = 2048
H = 2816
E = 8
CAP = 2048
DPAD = 2048
ROWLEN = DPAD + D
NG = TOKC // 128
NKD = D // 128
NKH = H // 128
NQ = TOKC // 512

NB = 128
JD = 64
KS = 16
NKP = KS // 2
TQW = 512
AW = TQW + KS

NH_K = NKH * NQ
NDC = 8
DCW = 256
MM_PER_PHASE = 2 * NH_K + NDC * 16
HV_PER_PHASE = NH_K + NDC * 16
EV_PER_PHASE = NH_K
YW_PER_PHASE = 16 * 64


def mm_base(which):
    return which * MM_PER_PHASE


def hv_base(which):
    return which * HV_PER_PHASE


def ev_base(which):
    return which * EV_PER_PHASE


def w1_wait(which, hc):
    return 16 * 8 * (which * 11 + hc // 2 + 1)


def w2_wait(which, dc):
    return 16 * 4 * (which * 4 + dc // 2 + 1)


RV_STATS0 = NQ + NG
KV_CUM0 = 2 * NG
KV_PRE0 = KV_CUM0 + 4
KV_TILE = 14
KV_DONE = KV_PRE0 + KV_TILE * NG

ZW = TTOK + TOKC + 128


class Chain:
    def __init__(self, nc, name, ctx, step=1):
        self.sem = ctx.enter_context(nc.semaphore(name))
        self.n = 0
        self.step = step

    def bump(self, instr, k=1):
        self.n += self.step * k
        instr.then_inc(self.sem, self.step * k)
        return self.n

    def wait(self, eng, n=None):
        eng.wait_ge(self.sem, self.n if n is None else n)


def build_nc():
    nc = bass.Bass("TRN2", target_bir_lowering=False, debug=False)

    x_in = nc.dram_tensor("x_in", [TOKC, D], F32, kind="ExternalInput").ap()
    wg_in = nc.dram_tensor("wg_in", [D, E], F32, kind="ExternalInput").ap()
    sw1_in = nc.dram_tensor("sw1_in", [D, H], F32, kind="ExternalInput").ap()
    sw3_in = nc.dram_tensor("sw3_in", [D, H], F32, kind="ExternalInput").ap()
    sw2_in = nc.dram_tensor("sw2_in", [H, D], F32, kind="ExternalInput").ap()
    we1_in = nc.dram_tensor("we1_in", [D, H], F32, kind="ExternalInput").ap()
    we3_in = nc.dram_tensor("we3_in", [D, H], F32, kind="ExternalInput").ap()
    we2_in = nc.dram_tensor("we2_in", [H, D], F32, kind="ExternalInput").ap()
    wat_in = nc.dram_tensor("wat_in", [NB, 128, NKP, JD], F32, kind="ExternalInput").ap()
    cols_in = nc.dram_tensor("cols_in", [1, NB], I32, kind="ExternalInput").ap()
    rows_in = nc.dram_tensor("rows_in", [1, NB], I32, kind="ExternalInput").ap()
    ident_in = nc.dram_tensor("ident_in", [8, 8], F32, kind="ExternalInput").ap()
    ltri_in = nc.dram_tensor("ltri_in", [128, 128], F32, kind="ExternalInput").ap()
    iota8_in = nc.dram_tensor("iota8_in", [128, E], F32, kind="ExternalInput").ap()
    revi_in = nc.dram_tensor("revi_in", [128, E], F32, kind="ExternalInput").ap()
    iotar_in = nc.dram_tensor("iotar_in", [128, 128], F32, kind="ExternalInput").ap()
    iotap_in = nc.dram_tensor("iotap_in", [128, 1], F32, kind="ExternalInput").ap()
    pconst_in = nc.dram_tensor("pconst_in", [128, 1], U32, kind="ExternalInput").ap()
    aidx_in = nc.dram_tensor("aidx_in", [128, 64], U32, kind="ExternalInput").ap()
    cme_in = nc.dram_tensor("cme_in", [1, 4], I32, kind="ExternalInput").ap()
    zpad_in = nc.dram_tensor("zpad_in", [1, ZW], F32, kind="ExternalInput")
    tokid_in = nc.dram_tensor("tokid_in", [128, NG], F32, kind="ExternalInput").ap()
    locpad_in = nc.dram_tensor("locpad_in", [128, NG], F32, kind="ExternalInput").ap()

    y_out = nc.dram_tensor("y_out", [TOKC, D], F32, kind="ExternalOutput").ap()

    ag1_in = nc.dram_tensor("ag1_in", [TOKC, ROWLEN], BF16)
    ag1_out = nc.dram_tensor("ag1_out", [TTOK, ROWLEN], BF16)
    ag2_in = nc.dram_tensor("ag2_in", [1, E], F32)
    ag2_out = nc.dram_tensor("ag2_out", [NCORE, E], F32)
    zsc = nc.dram_tensor("zsc", [1, ZW], F32)
    rbuf = nc.dram_tensor("rbuf", [1, ZW], F32)
    ag4_in = nc.dram_tensor("ag4_in", [CAP, D], BF16)
    ag4_out = nc.dram_tensor("ag4_out", [NCORE * CAP, D], BF16)
    ag5_in = nc.dram_tensor("ag5_in", [NB * JD, TQW], BF16)
    ag5_out = nc.dram_tensor("ag5_out", [NCORE * NB * JD, TQW], BF16)
    ysh_hbm = nc.dram_tensor("ysh_hbm", [TOKC, D], BF16)
    wattb = nc.dram_tensor("wattb", [NB, 128, NKP, JD], BF16)
    oall = nc.dram_tensor("oall", [NB, JD, TQW], BF16)
    slot_hbm = nc.dram_tensor("slot_hbm", [1, TOKC], I16)

    ag1_flat = ag1_out.ap().rearrange("r c -> (r c)").unsqueeze(0)
    ag5_flat = ag5_out.ap().rearrange("r c -> (r c)").unsqueeze(0)

    with ExitStack() as ctx:
        def S(name, shape, dt):
            return ctx.enter_context(nc.sbuf_tensor(name, shape, dt))

        def P(name, shape, dt):
            return ctx.enter_context(nc.psum_tensor(name, shape, dt))

        def CH(name, step=1):
            return Chain(nc, name, ctx, step)

        arena_a = S("arena_a", [128, NKH * TOKC], BF16)
        arena_b = S("arena_b", [128, 32768], BF16)
        af32 = arena_a[:, :].bitcast(F32)
        xs = af32[:, 0:2 * D]
        xcb = af32[:, 2 * D:4 * D]
        sqs = af32[:, 4 * D:5 * D]
        xnst = arena_a[:, 10 * D:10 * D + 2 * ROWLEN]
        lgfm = af32[0:8, 0:TOKC]
        A6 = 32768
        yout_b = arena_a[:, 0:A6]
        xg = arena_a[:, A6:A6 + 3 * AW]
        wab = arena_a[:, A6 + 3 * AW:A6 + 3 * AW + 3 * NKP * JD]
        oev = arena_a[0:64, A6 + 3 * AW + 3 * NKP * JD:A6 + 3 * AW + 3 * NKP * JD + 2 * TQW]
        x7 = af32[:, 0:2 * D]
        acc7 = af32[:, 2 * D:4 * D]
        out7 = af32[:, 4 * D:6 * D]
        att7 = arena_a[:, 12 * D:14 * D]
        ysh7 = arena_a[:, 14 * D:16 * D]

        lnsc = S("lnsc", [128, 16], F32)
        w1sl = S("w1sl", [128, 2 * NKD * 128], BF16)
        w3sl = S("w3sl", [128, 2 * NKD * 128], BF16)
        w2sl = S("w2sl", [128, 2 * NKH * DCW], BF16)
        w2f32 = w2sl[:, :].bitcast(F32)
        idxf = w2f32[0:16, 0:128]
        wgs = S("wgs", [128, NKD * E], BF16)
        scr = S("scr", [128, 2 * 512], F32)
        ost = S("ost", [128, 2 * DCW], BF16)

        ltm = S("ltm", [128, NG * E], F32)
        r8a = S("r8a", [128, E], F32)
        r8b = S("r8b", [128, E], F32)
        eidx_all = S("eidx_all", [128, NG], F32)
        gate_all = S("gate_all", [128, NG], F32)
        oh_all = S("oh_all", [128, NG * E], F32)
        pos_all = S("pos_all", [128, NG], F32)
        gscale = S("gscale", [128, NG], F32)
        slot_i16 = S("slot_i16", [128, NG], I16)
        slot_u32 = S("slot_u32", [128, NG], U32)
        tokid_sb = S("tokid_sb", [128, NG], F32)
        locpad_sb = S("locpad_sb", [128, NG], F32)
        base_sb = S("base_sb", [1, 2 * E], F32)
        ca_sb = S("ca_sb", [NCORE, E], F32)
        selv = S("selv", [NCORE, 1], F32)
        cmeb = S("cmeb", [16, 1], F32)
        cmef = S("cmef", [1, 1], F32)
        thpf = S("thpf", [1, 2 * E], F32)
        thbc = S("thbc", [128, 2 * E], F32)
        tcol = S("tcol", [128, 8], F32)

        ids = S("ids", [8, 8], F32)
        ltri = S("ltri", [128, 128], F32)
        iota8 = S("iota8", [128, E], F32)
        revi = S("revi", [128, E], F32)
        iotar = S("iotar", [128, 128], F32)
        iotap = S("iotap", [128, 1], F32)
        ones1 = S("ones1", [1, 128], F32)
        ones128 = S("ones128", [128, 1], F32)
        cme_sb = S("cme_sb", [1, 4], I32)
        pc_sb = S("pc_sb", [128, 1], U32)
        aidx_sb = S("aidx_sb", [128, 64], U32)

        idx16 = S("idx16", [128, 128], I16)
        sloti = S("sloti", [128, 128], I16)

        colsr = S("colsr", [1, NB], I32)
        rowsr = S("rowsr", [1, NB], I32)
        colf = S("colf", [1, NB], F32)
        rrf = S("rrf", [1, NB], F32)
        colbc = S("colbc", [128, NB], F32)
        colrep = S("colrep", [128, NB], U32)
        rowp = S("rowp", [128, 1], F32)
        smat = S("smat", [128, 128], BF16)

        pa0 = P("pa0", [128, 512], F32)
        pa1 = P("pa1", [128, 512], F32)
        pb0 = P("pb0", [128, 512], F32)
        pb1 = P("pb1", [128, 512], F32)
        py0 = P("py0", [128, 512], F32)
        py1 = P("py1", [128, 512], F32)
        pt0 = P("pt0", [128, 512], F32)
        pt1 = P("pt1", [128, 512], F32)
        pas = [pa0, pa1]
        pbs = [pb0, pb1]
        pyb = [py0, py1]

        c_init = CH("c_init", 16)
        c_dx = [CH("c_dx0", 16), CH("c_dx1", 16)]
        c_lv = CH("c_lv")
        c_ls = CH("c_ls")
        c_xnw = [CH("c_xnw0", 16), CH("c_xnw1", 16)]
        c_cc = CH("c_cc")
        c_fm = CH("c_fm", 16)
        c_pv = CH("c_pv")
        c_pt = CH("c_pt")
        c_pg = CH("c_pg")
        c_sc = CH("c_sc", 16)
        c_gm = CH("c_gm", 16)
        c_wg = CH("c_wg", 16)
        c_wt = CH("c_wt", 16)
        c_sm2 = CH("c_sm2", 16)
        c_rt = CH("c_rt")
        c_rv = CH("c_rv")
        c_rs = CH("c_rs")
        c_ct = CH("c_ct")
        c_cv = CH("c_cv")
        c_kt = CH("c_kt")
        c_kv = CH("c_kv")
        c_w1 = [CH("c_w10", 16), CH("c_w11", 16)]
        c_w2 = [CH("c_w20", 16), CH("c_w21", 16)]
        c_mm = CH("c_mm")
        c_ev = CH("c_ev")
        c_hv = CH("c_hv")
        c_yw = [CH("c_yw0", 16), CH("c_yw1", 16)]
        c_vl = CH("c_vl")
        c_xd = CH("c_xd", 16)
        c_ga = [CH("c_ga0", 16), CH("c_ga1", 16), CH("c_ga2", 16)]
        c_wa = [CH("c_wa0", 16), CH("c_wa1", 16), CH("c_wa2", 16)]
        c_am = CH("c_am")
        c_ae = CH("c_ae")
        c_ao = [CH("c_ao0", 16), CH("c_ao1", 16)]
        c_or = CH("c_or", 16)
        c_sm = CH("c_sm")
        c_sv = CH("c_sv")
        c_a5 = CH("c_a5", 16)
        c_mg = CH("c_mg", 16)
        c_si = CH("c_si", 16)
        c_px = [CH("c_px0", 16), CH("c_px1", 16)]
        c_pys = [CH("c_pys0", 16), CH("c_pys1", 16)]
        c_pat = [CH("c_pat0", 16), CH("c_pat1", 16)]
        c_as = CH("c_as")
        c_out = [CH("c_out0", 16), CH("c_out1", 16)]

        def emit_ffn_tensor(t, which):
            bm = mm_base(which)
            bh = hv_base(which)
            if which == 0:
                c_fm.wait(t, 16 * NKD)
            else:
                c_xd.wait(t, 16 * 4)
            for hc in range(NKH):
                for q in range(NQ):
                    k = hc * 4 + q
                    c_w1[hc % 2].wait(t, w1_wait(which, hc))
                    if k >= 2:
                        c_hv.wait(t, bh + k - 1)
                    pa = pas[q % 2]
                    pb = pbs[q % 2]

                    def rhs(g, q=q):
                        if which == 0:
                            return arena_b[:, g * TOKC + q * 512: g * TOKC + q * 512 + 512]
                        return arena_b[:, q * 8192 + g * 512: q * 8192 + g * 512 + 512]
                    for g in range(NKD):
                        mi = t.matmul(pa[:, :],
                                      w1sl[:, bass.ts(hc % 2, NKD * 128)][:, bass.ts(g, 128)],
                                      rhs(g),
                                      start=(g == 0), stop=(g == NKD - 1))
                    c_mm.bump(mi)
                    for g in range(NKD):
                        mi = t.matmul(pb[:, :],
                                      w3sl[:, bass.ts(hc % 2, NKD * 128)][:, bass.ts(g, 128)],
                                      rhs(g),
                                      start=(g == 0), stop=(g == NKD - 1))
                    c_mm.bump(mi)
            bh2 = bh + NH_K
            for dc in range(NDC):
                for sg in range(16):
                    m = dc * 16 + sg
                    c_w2[dc % 2].wait(t, w2_wait(which, dc))
                    if m >= 2:
                        c_hv.wait(t, bh2 + m - 1)
                    py = pyb[m % 2]
                    for hc in range(NKH):
                        mi = t.matmul(py[:, 0:DCW],
                                      arena_a[:, hc * TOKC + sg * 128: hc * TOKC + sg * 128 + 128],
                                      w2sl[:, bass.ts(dc % 2, NKH * DCW)][:, bass.ts(hc, DCW)],
                                      start=(hc == 0), stop=(hc == NKH - 1))
                    c_mm.bump(mi)

        def emit_ffn_scalar(sc, which):
            bm = mm_base(which)
            bh = hv_base(which)
            for hc in range(NKH):
                for q in range(NQ):
                    k = hc * 4 + q
                    c_mm.wait(sc, bm + 2 * k + 1)
                    if k >= 2:
                        c_hv.wait(sc, bh + k - 1)
                    c_ev.bump(sc.activation(scr[:, bass.ts(k % 2, 512)], pas[q % 2][:, :], AF.Silu))

        def emit_ffn_vector(v, which):
            bm = mm_base(which)
            be = ev_base(which)
            for hc in range(NKH):
                for q in range(NQ):
                    k = hc * 4 + q
                    c_ev.wait(v, be + k + 1)
                    c_mm.wait(v, bm + 2 * k + 2)
                    c_hv.bump(v.tensor_mul(arena_a[:, hc * TOKC + q * 512: hc * TOKC + (q + 1) * 512],
                                           scr[:, bass.ts(k % 2, 512)], pbs[q % 2][:, :]))
            bm2 = bm + 2 * NH_K
            byw = which * YW_PER_PHASE
            for dc in range(NDC):
                for sg in range(16):
                    m = dc * 16 + sg
                    c_mm.wait(v, bm2 + m + 1)
                    if m >= 2 or which == 1:
                        c_yw[m % 2].wait(v, byw + 16 * (m // 2))
                    c_hv.bump(v.tensor_copy(ost[:, bass.ts(m % 2, DCW)], pyb[m % 2][:, 0:DCW]))

        def emit_ffn_sync(s, out_ap, which):
            bh2 = hv_base(which) + NH_K
            for dc in range(NDC):
                for sg in range(16):
                    m = dc * 16 + sg
                    c_hv.wait(s, bh2 + m + 1)
                    c_yw[m % 2].bump(s.dma_start(
                        out=out_ap[bass.ts(sg, 128), bass.ts(dc, DCW)],
                        in_=ost[:, bass.ts(m % 2, DCW)]))

        def emit_ffn_gpsimd_weights(g, w1a, w3a, w2a, which):
            bm = mm_base(which)
            for hc in range(NKH):
                if hc >= 2:
                    c_mm.wait(g, bm + 8 * (hc - 1))
                for wsl, wa in ((w1sl, w1a), (w3sl, w3a)):
                    for part in range(4):
                        r0 = part * (NKD // 4)
                        c_w1[hc % 2].bump(g.dma_start(
                            out=wsl[:, bass.ts(hc % 2, NKD * 128)]
                                .rearrange("p (gg h) -> p gg h", gg=NKD)[:, r0:r0 + NKD // 4, :],
                            in_=wa[:, 128 * hc:128 * (hc + 1)]
                                .rearrange("(gg p) h -> p gg h", p=128)[:, r0:r0 + NKD // 4, :]))
            for dc in range(NDC):
                if dc >= 2:
                    c_mm.wait(g, bm + 2 * NH_K + 16 * (dc - 1))
                for part in range(4):
                    h0 = part * 6
                    h1 = min(NKH, h0 + 6)
                    c_w2[dc % 2].bump(g.dma_start(
                        out=w2sl[:, bass.ts(dc % 2, NKH * DCW)]
                            .rearrange("p (hh d) -> p hh d", hh=NKH)[:, h0:h1, :],
                        in_=w2a[:, DCW * dc:DCW * (dc + 1)]
                            .rearrange("(hh p) d -> p hh d", p=128)[:, h0:h1, :]))

        block = ctx.enter_context(nc.Block())

        @block.sync
        def _(s):
            for ap_in, ap_sb in [(ident_in, ids), (ltri_in, ltri), (iota8_in, iota8),
                                 (revi_in, revi), (iotar_in, iotar), (iotap_in, iotap),
                                 (cols_in, colsr), (rows_in, rowsr), (cme_in, cme_sb),
                                 (pconst_in, pc_sb), (aidx_in, aidx_sb),
                                 (tokid_in, tokid_sb), (locpad_in, locpad_sb)]:
                c_init.bump(s.dma_start(out=ap_sb[:, :], in_=ap_in[:, :]))
            for g in range(NG):
                if g >= 2:
                    c_lv.wait(s, 6 * (g - 2) + 3)
                c_dx[g % 2].bump(s.dma_start(out=xs[:, bass.ts(g % 2, D)],
                                             in_=x_in[bass.ts(g, 128), :]))
                if g >= 1:
                    gg = g - 1
                    c_lv.wait(s, 6 * gg + 6)
                    c_xnw[gg % 2].bump(s.dma_start(out=ag1_in.ap()[bass.ts(gg, 128), :],
                                                   in_=xnst[:, bass.ts(gg % 2, ROWLEN)]))
            gg = NG - 1
            c_lv.wait(s, 6 * gg + 6)
            c_xnw[gg % 2].bump(s.dma_start(out=ag1_in.ap()[bass.ts(gg, 128), :],
                                           in_=xnst[:, bass.ts(gg % 2, ROWLEN)]))
            s.wait_ge(c_xnw[0].sem, 16 * 8)
            s.wait_ge(c_xnw[1].sem, 16 * 8)
            for g in range(NKD):
                c_fm.bump(s.dma_start_transpose(arena_b[:, bass.ts(g, TOKC)],
                                                ag1_in.ap()[:, DPAD + 128 * g: DPAD + 128 * (g + 1)]))
                s.wait_ge(c_fm.sem, c_fm.n)
            c_cc.wait(s, 2)
            c_sm2.bump(s.dma_start(out=ca_sb[:, :], in_=ag2_out.ap()[:, :]))
            c_kv.wait(s, KV_DONE)
            for gg8 in range(8):
                c_sm2.bump(s.dma_start(
                    out=slot_hbm.ap().rearrange("o (q c g) -> o q c g", q=16, c=16, g=8)[:, :, :, gg8].squeeze(0),
                    in_=slot_i16[16 * gg8:16 * (gg8 + 1), :],
                    allow_slow_non_contiguous=True))
            emit_ffn_sync(s, ysh_hbm.ap(), 0)
            emit_ffn_sync(s, ag4_in.ap(), 1)
            c_wt.wait(s, 16 * 16)
            c_mm.wait(s, 2 * MM_PER_PHASE)
            for b in range(NB + 2):
                if b < NB:
                    buf = b % 3
                    if b >= 3:
                        c_am.wait(s, b - 2)
                    c_wa[buf].bump(s.dma_start(
                        out=wab[:, bass.ts(buf, NKP * JD)],
                        in_=wattb.ap()[b, :, :, :].rearrange("p k i -> p (k i)")))
                if b >= 2:
                    bo = b - 2
                    c_ae.wait(s, bo + 1)
                    c_ao[bo % 2].bump(s.dma_start(out=oall.ap()[bo, :, :],
                                                  in_=oev[:, bass.ts(bo % 2, TQW)]))
            c_cc.wait(s, 5)
            c_sm2.wait(s, 16 * 9)
            c_si.bump(s.dma_start(out=sloti[0:16, :],
                                  in_=slot_hbm.ap().rearrange("o (q c) -> (o q) c", q=16)))
            for g in range(NG):
                if g >= 2:
                    c_as.wait(s, 3 * (g - 2) + 3)
                c_px[g % 2].bump(s.dma_start(out=x7[:, bass.ts(g % 2, D)],
                                             in_=x_in[bass.ts(g, 128), :]))
                if g >= 2:
                    c_as.wait(s, 3 * (g - 2) + 2)
                c_pys[g % 2].bump(s.dma_start(out=ysh7[:, bass.ts(g % 2, D)],
                                              in_=ysh_hbm.ap()[bass.ts(g, 128), :]))
                if g >= 2:
                    go = g - 2
                    c_as.wait(s, 3 * go + 3)
                    c_out[go % 2].bump(s.dma_start(out=y_out[bass.ts(go, 128), :],
                                                   in_=out7[:, bass.ts(go % 2, D)]))
            for go in (NG - 2, NG - 1):
                c_as.wait(s, 3 * go + 3)
                c_out[go % 2].bump(s.dma_start(out=y_out[bass.ts(go, 128), :],
                                               in_=out7[:, bass.ts(go % 2, D)]))
            s.wait_ge(c_out[0].sem, 16 * 8)
            s.wait_ge(c_out[1].sem, 16 * 8)

        @block.vector
        def _(v):
            v.memset(ones1[:, :], 1.0)
            v.memset(ones128[:, :], 1.0)
            v.memset(base_sb[:, :], 0.0)
            for g in range(NG):
                bi = g % 2
                c_dx[bi].wait(v, 16 * (g // 2 + 1))
                xsg = xs[:, bass.ts(bi, D)]
                xcg = xcb[:, bass.ts(bi, D)]
                sl = lnsc[:, 8 * bi: 8 * bi + 8]
                c_lv.bump(v.reduce_sum(sl[:, 0:1], xsg, axis=AX.X, opt_output=False))
                c_lv.wait(v)
                c_lv.bump(v.tensor_scalar(sl[:, 1:2], sl[:, 0:1], 1.0 / D, None, AL.mult))
                c_lv.wait(v)
                if g >= 2:
                    c_ls.wait(v, 2 * (g - 2) + 1)
                c_lv.bump(v.tensor_scalar(xcg, xsg, sl[:, 1:2], None, AL.subtract))
                c_ls.wait(v, 2 * g + 1)
                c_lv.bump(v.tensor_scalar(sl[:, 3:4], sl[:, 2:3], 1.0 / D, 1e-5, AL.mult, AL.add))
                c_ls.wait(v, 2 * g + 2)
                c_lv.bump(v.reciprocal(sl[:, 5:6], sl[:, 4:5]))
                c_lv.wait(v)
                if g < 2:
                    v.memset(xnst[:, bass.ts(bi, ROWLEN)][:, 0:DPAD], 0.0)
                if g >= 2:
                    c_xnw[bi].wait(v, 16 * (g // 2))
                c_lv.bump(v.tensor_scalar(xnst[:, bass.ts(bi, ROWLEN)][:, DPAD:DPAD + D],
                                          xcg, sl[:, 5:6], None, AL.mult))
            c_init.wait(v, c_init.n)
            c_pv.bump(v.tensor_copy(colf[:, :], colsr[:, :]))
            c_pv.bump(v.tensor_copy(rrf[:, :], rowsr[:, :]))
            c_pv.bump(v.tensor_copy(cmef[:, :], cme_sb[0:1, 0:1]))
            c_pt.wait(v, 3)
            c_pv.bump(v.tensor_copy(colbc[:, :], pt0[0:128, 0:NB]))
            c_pv.bump(v.tensor_copy(rowp[:, :], pt1[0:128, 0:1]))
            c_pv.bump(v.tensor_copy(cmeb[:, :], pt1[0:16, 4:5]))
            c_pv.wait(v)
            c_pv.bump(v.tensor_scalar(smat[:, :], iotar[:, :], rowp[:, 0:1], None, AL.is_equal))
            c_pg.bump(v.tensor_copy(colrep[:, :], colbc[:, :]))
            c_pg.wait(v, 1)
            c_pg.bump(v.tensor_scalar(colrep[:, :], colrep[:, :], 18, None,
                                      AL.logical_shift_left))
            c_pg.wait(v, 2)
            c_pg.bump(v.tensor_tensor(colrep[:, :], colrep[:, :],
                                      pc_sb[:, 0:1].broadcast_to([128, NB]), AL.bitwise_or))
            for q in range(NQ):
                c_rt.wait(v, q + 1)
                c_rv.bump(v.tensor_copy(lgfm[0:8, bass.ts(q, 512)], pt0[0:8, 0:512]))
            for c in range(NG):
                c_rt.wait(v, NQ + c + 1)
                c_rv.bump(v.tensor_copy(ltm[:, bass.ts(c, E)], pt1[0:128, 0:8]))
            for c in range(NG):
                lt = ltm[:, bass.ts(c, E)]
                c_rv.bump(v.tensor_max(r8a[:, 0:4], lt[:, 0:4], lt[:, 4:8]))
                c_rv.wait(v)
                c_rv.bump(v.tensor_max(r8a[:, 4:6], r8a[:, 0:2], r8a[:, 2:4]))
                c_rv.wait(v)
                c_rv.bump(v.tensor_max(tcol[:, 0:1], r8a[:, 4:5], r8a[:, 5:6]))
                c_rv.wait(v)
                c_rv.bump(v.tensor_scalar(r8a[:, 0:8], lt, tcol[:, 0:1], None, AL.is_equal))
                c_rv.wait(v)
                c_rv.bump(v.tensor_mul(r8b[:, 0:8], r8a[:, 0:8], revi[:, :]))
                c_rv.wait(v)
                c_rv.bump(v.tensor_max(r8b[:, 0:4], r8b[:, 0:4], r8b[:, 4:8]))
                c_rv.wait(v)
                c_rv.bump(v.tensor_max(r8b[:, 0:2], r8b[:, 0:2], r8b[:, 2:4]))
                c_rv.wait(v)
                c_rv.bump(v.tensor_max(tcol[:, 1:2], r8b[:, 0:1], r8b[:, 1:2]))
                c_rv.wait(v)
                c_rv.bump(v.tensor_scalar(eidx_all[:, c:c + 1], tcol[:, 1:2], -1.0, 8.0, AL.mult, AL.add))
                c_rv.bump(v.tensor_scalar(tcol[:, 2:3], tcol[:, 0:1], -1.0, None, AL.mult))
                c_rv.wait(v)
                c_rs.wait(v, c + 1)
                c_rv.bump(v.reduce_sum(tcol[:, 3:4], r8b[:, 0:8], axis=AX.X, opt_output=False))
                c_rv.wait(v)
                c_kv.bump(v.reciprocal(gate_all[:, c:c + 1], tcol[:, 3:4]))
                c_kv.wait(v)
                c_kv.bump(v.tensor_scalar(oh_all[:, bass.ts(c, E)], iota8[:, :],
                                          eidx_all[:, c:c + 1], None, AL.is_equal))
            for c in range(NG):
                c_ct.wait(v, 2 * c + 1)
                c_cv.bump(v.tensor_copy(tcol[:, 0:8], pt0[0:128, 0:8]))
                c_cv.wait(v)
                c_cv.bump(v.tensor_mul(tcol[:, 0:8], tcol[:, 0:8], oh_all[:, bass.ts(c, E)]))
                c_cv.wait(v)
                c_cv.bump(v.reduce_sum(pos_all[:, c:c + 1], tcol[:, 0:8], axis=AX.X, opt_output=False))
                c_ct.wait(v, 2 * c + 2)
                c_cv.bump(v.tensor_add(base_sb[0:1, 0:E], base_sb[0:1, 0:E], pt1[0:1, 0:8]))
                c_cv.wait(v)
            c_sm2.wait(v, 16)
            c_kv.bump(v.tensor_tensor(selv[:, :], iotap[0:8, :], cmeb[0:8, :], AL.is_lt))
            c_kt.wait(v, 1)
            c_kv.bump(v.tensor_copy(thpf[0:1, 8:16], pt0[0:1, 0:8]))
            c_kv.wait(v)
            c_kv.bump(v.tensor_scalar(thpf[0:1, 0:8], thpf[0:1, 8:16], -1.0, float(CAP), AL.mult, AL.add))
            c_kt.wait(v, 2)
            c_kv.bump(v.tensor_copy(thbc[:, :], pt0[0:128, 0:16]))
            c_kv.wait(v)
            for c in range(NG):
                ohc = oh_all[:, bass.ts(c, E)]
                c_kv.bump(v.tensor_mul(r8a[:, 0:8], thbc[:, 0:8], ohc))
                c_kv.wait(v)
                c_kv.bump(v.reduce_sum(tcol[:, 0:1], r8a[:, 0:8], axis=AX.X, opt_output=False))
                c_kv.wait(v)
                c_kv.bump(v.tensor_mul(r8a[:, 0:8], thbc[:, 8:16], ohc))
                c_kv.wait(v)
                c_kv.bump(v.reduce_sum(tcol[:, 1:2], r8a[:, 0:8], axis=AX.X, opt_output=False))
                c_kv.wait(v)
                c_kv.bump(v.tensor_tensor(tcol[:, 3:4], pos_all[:, c:c + 1], tcol[:, 0:1], AL.is_lt))
                c_kv.wait(v)
                c_kv.bump(v.tensor_mul(gscale[:, c:c + 1], gate_all[:, c:c + 1], tcol[:, 3:4]))
                c_kv.bump(v.scalar_tensor_tensor(tcol[:, 4:5], eidx_all[:, c:c + 1], float(CAP),
                                                 pos_all[:, c:c + 1], AL.mult, AL.add))
                c_kv.wait(v)
                c_kv.bump(v.tensor_add(tcol[:, 4:5], tcol[:, 4:5], tcol[:, 1:2]))
                c_kv.wait(v)
                c_kv.bump(v.tensor_scalar_min(tcol[:, 6:7], tcol[:, 4:5], float(TTOK - 1)))
                c_kv.wait(v)
                c_kv.bump(v.tensor_tensor(tcol[:, 5:6], tcol[:, 6:7], locpad_sb[:, c:c + 1],
                                          AL.subtract))
                c_kv.bump(v.tensor_copy(slot_i16[:, c:c + 1], tcol[:, 6:7]))
                c_kv.wait(v)
                c_kv.bump(v.tensor_mul(tcol[:, 5:6], tcol[:, 5:6], tcol[:, 3:4]))
                c_kv.wait(v)
                c_kv.bump(v.tensor_add(tcol[:, 5:6], tcol[:, 5:6], locpad_sb[:, c:c + 1]))
                c_kv.wait(v)
                c_kv.bump(v.tensor_copy(slot_u32[:, c:c + 1], tcol[:, 5:6]))
                c_kv.wait(v)
            c_gm.wait(v, 16 * 2)
            c_vl.bump(v.tensor_scalar(idxf[:, :], idxf[:, :], -1.0, None, AL.add))
            c_vl.wait(v)
            c_vl.bump(v.tensor_scalar_max(idxf[:, :], idxf[:, :], 0.0))
            c_vl.wait(v)
            c_vl.bump(v.tensor_copy(idx16[0:16, :], idxf[:, :]))
            emit_ffn_vector(v, 0)
            emit_ffn_vector(v, 1)
            for q in range(64):
                c_sm.wait(v, q + 1)
                c_sv.bump(v.tensor_copy(yout_b[:, bass.ts(q, 512)], pyb[q % 2][:, :]))
            c_mg.wait(v, 16 * 4)
            for g in range(NG):
                bi = g % 2
                c_pat[bi].wait(v, 16 * 4 * (g // 2 + 1))
                c_px[bi].wait(v, 16 * (g // 2 + 1))
                c_pys[bi].wait(v, 16 * (g // 2 + 1))
                if g >= 2:
                    c_out[g % 2].wait(v, 16 * (g // 2))
                c_as.bump(v.scalar_tensor_tensor(acc7[:, bass.ts(bi, D)],
                                                 arena_b[:, bass.ts(g, D)], gscale[:, g:g + 1],
                                                 att7[:, bass.ts(bi, D)], AL.mult, AL.add))
                c_as.wait(v)
                c_as.bump(v.tensor_add(acc7[:, bass.ts(bi, D)], acc7[:, bass.ts(bi, D)],
                                       ysh7[:, bass.ts(bi, D)]))
                c_as.wait(v)
                c_as.bump(v.tensor_add(out7[:, bass.ts(bi, D)], acc7[:, bass.ts(bi, D)],
                                       x7[:, bass.ts(bi, D)]))

        @block.scalar
        def _(sc):
            for g in range(NG):
                bi = g % 2
                sl = lnsc[:, 8 * bi: 8 * bi + 8]
                c_lv.wait(sc, 6 * g + 3)
                c_ls.bump(sc.activation(sqs[:, :], xcb[:, bass.ts(bi, D)], AF.Square,
                                        accum_out=sl[:, 2:3]))
                c_lv.wait(sc, 6 * g + 4)
                c_ls.bump(sc.activation(sl[:, 4:5], sl[:, 3:4], AF.Sqrt))
            for c in range(NG):
                c_rv.wait(sc, RV_STATS0 + 11 * c + 10)
                c_rs.bump(sc.activation(r8b[:, 0:8], ltm[:, bass.ts(c, E)], AF.Exp,
                                        bias=tcol[:, 2:3], scale=1.0))
            emit_ffn_scalar(sc, 0)
            emit_ffn_scalar(sc, 1)
            for b in range(NB):
                c_am.wait(sc, b + 1)
                if b >= 2:
                    c_ao[b % 2].wait(sc, 16 * (b // 2))
                c_ae.bump(sc.copy(oev[:, bass.ts(b % 2, TQW)], pas[b % 2][0:64, 0:TQW]))

        @block.tensor
        def _(t):
            c_pv.wait(t, 3)
            c_pt.bump(t.matmul(pt0[0:128, 0:NB], ones1[0:1, :], colf[0:1, :],
                               start=True, stop=True))
            c_pt.bump(t.transpose(pt1[0:128, 0:1], rrf[0:1, 0:NB], ids[0:1, 0:1]))
            c_pt.bump(t.matmul(pt1[0:16, 4:5], ones1[0:1, 0:16], cmef[0:1, 0:1],
                               start=True, stop=True))
            c_pv.wait(t, 6)
            c_fm.wait(t, 16 * NKD)
            c_wg.wait(t, 16 * 2)
            for q in range(NQ):
                if q >= 1:
                    c_rv.wait(t, q)
                for g in range(NKD):
                    mi = t.matmul(pt0[0:8, 0:512], wgs[:, bass.ts(g, E)],
                                  arena_b[:, bass.ts(g, TOKC)][:, bass.ts(q, 512)],
                                  start=(g == 0), stop=(g == NKD - 1))
                c_rt.bump(mi)
            for c in range(NG):
                c_rv.wait(t, NQ + c)
                c_rt.bump(t.transpose(pt1[0:128, 0:8], lgfm[0:8, bass.ts(c, 128)], ids[0:8, 0:8]))
            for c in range(NG):
                c_kv.wait(t, 2 * c + 2)
                if c >= 1:
                    c_cv.wait(t, 4 * (c - 1) + 4)
                mi = t.matmul(pt0[0:128, 0:8], ltri[:, :], oh_all[:, bass.ts(c, E)],
                              start=True, stop=False)
                mi = t.matmul(pt0[0:128, 0:8], ones1[0:1, :], base_sb[0:1, 0:E],
                              start=False, stop=True)
                c_ct.bump(mi)
                mi = t.matmul(pt1[0:1, 0:8], ones128[:, :], oh_all[:, bass.ts(c, E)],
                              start=True, stop=True)
                c_ct.bump(mi)
            c_cv.wait(t, 4 * NG)
            c_kv.wait(t, KV_CUM0 + 1)
            c_kt.bump(t.matmul(pt0[0:1, 0:8], selv[0:8, :], ca_sb[0:8, :], start=True, stop=True))
            c_kv.wait(t, KV_CUM0 + 3)
            c_kt.bump(t.matmul(pt0[0:128, 0:16], ones1[0:1, :], thpf[0:1, 0:16], start=True, stop=True))
            emit_ffn_tensor(t, 0)
            emit_ffn_tensor(t, 1)
            c_pg.wait(t, 3)
            c_ev.wait(t, 2 * EV_PER_PHASE)
            c_hv.wait(t, hv_base(1) + NH_K)
            for b in range(NB):
                buf = b % 3
                c_wa[buf].wait(t, 16 * (b // 3 + 1))
                c_ga[buf].wait(t, 16 * (b // 3 + 1))
                if b >= 2:
                    c_ae.wait(t, b - 1)
                ps = pas[b % 2]
                for kp in range(NKP):
                    mi = t.matmul(ps[0:64, 0:TQW],
                                  wab[:, bass.ts(buf, NKP * JD)][:, bass.ts(kp, JD)],
                                  xg[:, bass.ts(buf, AW)][:, 2 * kp:2 * kp + TQW],
                                  start=(kp == 0), stop=(kp == NKP - 1))
                c_am.bump(mi)
            c_or.wait(t, 16)
            c_pv.wait(t, 7)
            c_hv.wait(t, 2 * HV_PER_PHASE)
            for q in range(64):
                if q >= 2:
                    c_sv.wait(t, q - 1)
                mi = t.matmul(pyb[q % 2][:, :], smat[:, :], arena_b[:, bass.ts(q, 512)],
                              start=True, stop=True)
                c_sm.bump(mi)

        @block.gpsimd
        def _(g):
            c_sc.bump(g.dma_start(out=zsc.ap()[:, :], in_=zpad_in.ap()[:, :]))
            c_wg.bump(g.dma_start(
                out=wgs[:, :].rearrange("p (gg e) -> p gg e", gg=NKD)[:, 0:8, :],
                in_=wg_in.rearrange("(gg p) e -> p gg e", p=128)[:, 0:8, :]))
            c_wg.bump(g.dma_start(
                out=wgs[:, :].rearrange("p (gg e) -> p gg e", gg=NKD)[:, 8:16, :],
                in_=wg_in.rearrange("(gg p) e -> p gg e", p=128)[:, 8:16, :]))
            for cch in range(16):
                c_wt.bump(g.dma_start(out=wattb.ap()[8 * cch:8 * (cch + 1), :, :, :],
                                      in_=wat_in[8 * cch:8 * (cch + 1), :, :, :]))
            c_xnw[0].wait(g, 16 * 8)
            c_xnw[1].wait(g, 16 * 8)
            c_cc.bump(g.collective_compute(
                "AllGather", AL.bypass, replica_groups=[list(range(NCORE))],
                ins=[ag1_in.ap().opt()], outs=[ag1_out.ap().opt()]))
            c_cv.wait(g, 4 * NG)
            c_gm.bump(g.dma_start(out=ag2_in.ap()[:, :], in_=base_sb[0:1, 0:E]))
            g.wait_ge(c_gm.sem, c_gm.n)
            c_cc.bump(g.collective_compute(
                "AllGather", AL.bypass, replica_groups=[list(range(NCORE))],
                ins=[ag2_in.ap().opt()], outs=[ag2_out.ap().opt()]))
            c_kv.wait(g, KV_DONE)
            g.wait_ge(c_sc.sem, 16)
            for c in range(NG):
                c_sc.bump(g.indirect_dma_start(
                    out=zsc.ap(),
                    out_offset=bass.IndirectOffsetOnAxis(ap=slot_u32[:, c:c + 1], axis=1),
                    in_=tokid_sb[:, c:c + 1], in_offset=None))
            g.wait_ge(c_sc.sem, 16 * 17)
            c_cc.bump(g.collective_compute(
                "AllReduce", AL.add, replica_groups=[list(range(NCORE))],
                ins=[zsc.ap().opt()], outs=[rbuf.ap().opt()]))
            c_cc.wait(g, 3)
            with g.register("ro") as ro:
                g.load(ro, cme_sb[0:1, 1:2])
                so = g.snap(ro, min_val=0, max_val=(NCORE - 1) * TOKC)
                c_gm.bump(g.dma_start(
                    out=idxf[:, :],
                    in_=rbuf.ap()[0:1, bass.DynSlice(so, TOKC)]
                        .rearrange("a (c r) -> (a r) c", r=16),
                    allow_slow_non_contiguous=True))
            c_vl.wait(g, 3)
            for r in range(1, 8):
                c_gm.bump(g.dma_start(out=idx16[16 * r:16 * (r + 1), :], in_=idx16[0:16, :]))
            g.wait_ge(c_gm.sem, c_gm.n)
            g.load_library(library_config.mlp)
            emit_ffn_gpsimd_weights(g, sw1_in, sw3_in, sw2_in, 0)
            c_mm.wait(g, mm_base(0) + 2 * NH_K)
            c_cc.wait(g, 1)
            for kk in range(4):
                c_xd.bump(g.dma_gather(
                    arena_b[:, 8192 * kk:8192 * (kk + 1)].rearrange("p (a b) -> p a b", b=512),
                    ag1_out.ap()[:, DPAD:DPAD + D],
                    idx16[:, 32 * kk:32 * (kk + 1)],
                    num_idxs=512, num_idxs_reg=512, elem_size=D, elem_step=ROWLEN,
                    transpose=True))
            emit_ffn_gpsimd_weights(g, we1_in, we3_in, we2_in, 1)
            c_yw[0].wait(g, 2 * YW_PER_PHASE)
            c_yw[1].wait(g, 2 * YW_PER_PHASE)
            c_cc.bump(g.collective_compute(
                "AllGather", AL.bypass, replica_groups=[list(range(NCORE))],
                ins=[ag4_in.ap().opt()], outs=[ag4_out.ap().opt()]))
            c_pg.wait(g, 3)
            c_mm.wait(g, 2 * MM_PER_PHASE)
            for b in range(NB):
                buf = b % 3
                if b >= 3:
                    c_am.wait(g, b - 2)
                c_ga[buf].bump(g.indirect_dma_start(
                    out=xg[:, bass.ts(buf, AW)], out_offset=None,
                    in_=ag1_flat,
                    in_offset=bass.IndirectOffsetOnAxis(ap=colrep[:, b:b + 1], axis=1)))
            c_mm.wait(g, mm_base(1) + 2 * NH_K)
            c_ao[0].wait(g, 16 * 64)
            c_ao[1].wait(g, 16 * 64)
            c_or.bump(g.dma_start(out=arena_b[0:128, 0:64 * 512],
                                  in_=oall.ap().rearrange("b i t -> b (i t)")))
            for rs in range(8):
                c_sv.wait(g, 64)
                c_a5.bump(g.dma_start(
                    out=ag5_in.ap().rearrange("(r i) t -> r i t", i=JD)[16 * rs:16 * (rs + 1), :, :],
                    in_=yout_b[16 * rs:16 * (rs + 1), 0:A6].rearrange("r (i t) -> r i t", t=512)))
            g.wait_ge(c_a5.sem, 16 * 8)
            c_cc.bump(g.collective_compute(
                "AllGather", AL.bypass, replica_groups=[list(range(NCORE))],
                ins=[ag5_in.ap().opt()], outs=[ag5_out.ap().opt()]))
            c_si.wait(g, 16)
            for r in range(1, 8):
                c_gm.bump(g.dma_start(out=sloti[16 * r:16 * (r + 1), :], in_=sloti[0:16, :]))
            g.wait_ge(c_gm.sem, c_gm.n)
            c_cc.wait(g, 4)
            c_sm.wait(g, 64)
            for kk in range(4):
                c_mg.bump(g.dma_gather(
                    arena_b[:, :].rearrange("p (a b) -> p a b", a=16)[:, 4 * kk:4 * (kk + 1), :],
                    ag4_out.ap()[:, :],
                    sloti[:, 32 * kk:32 * (kk + 1)],
                    num_idxs=512, num_idxs_reg=512, elem_size=D,
                    transpose=False))
            c_cc.wait(g, 5)
            for gi in range(NG):
                bi = gi % 2
                if gi >= 2:
                    c_as.wait(g, 3 * (gi - 2) + 1)
                for tq in range(4):
                    c_pat[bi].bump(g.indirect_dma_start(
                        out=att7[:, bass.ts(bi, D)][:, bass.ts(tq, 512)], out_offset=None,
                        in_=ag5_flat,
                        in_offset=bass.IndirectOffsetOnAxis(
                            ap=aidx_sb[:, 16 * tq + gi:16 * tq + gi + 1], axis=1)))
            g.wait_ge(c_out[0].sem, 16 * 8)
            g.wait_ge(c_out[1].sem, 16 * 8)

    return nc


def _consts():
    ident = np.eye(8, dtype=np.float32)
    ltri = (np.arange(128)[:, None] < np.arange(128)[None, :]).astype(np.float32)
    iota8 = np.tile(np.arange(E, dtype=np.float32)[None, :], (128, 1))
    revi = 8.0 - iota8
    iotar = np.tile(np.arange(128, dtype=np.float32)[None, :], (128, 1))
    iotap = np.arange(128, dtype=np.float32)[:, None].copy()
    return ident, ltri, iota8, revi, iotar, iotap


def kernel(L0_attn_x, L0_attn_block_values, L0_attn_cols, L0_attn_rows,
           L0_ffn_Wg, L0_ffn_W1, L0_ffn_W3, L0_ffn_W2,
           L0_ffn_sw1, L0_ffn_sw3, L0_ffn_sw2):
    x = np.ascontiguousarray(np.asarray(L0_attn_x, dtype=np.float32)).reshape(TTOK, D)
    bv = np.ascontiguousarray(np.asarray(L0_attn_block_values, dtype=np.float32))
    cols = np.ascontiguousarray(np.asarray(L0_attn_cols, dtype=np.int32).reshape(1, NB))
    rows = np.ascontiguousarray(np.asarray(L0_attn_rows, dtype=np.int32).reshape(1, NB))
    wg = np.ascontiguousarray(np.asarray(L0_ffn_Wg, dtype=np.float32))
    w1 = np.asarray(L0_ffn_W1, dtype=np.float32)
    w3 = np.asarray(L0_ffn_W3, dtype=np.float32)
    w2 = np.asarray(L0_ffn_W2, dtype=np.float32)
    sw1 = np.ascontiguousarray(np.asarray(L0_ffn_sw1, dtype=np.float32))
    sw3 = np.ascontiguousarray(np.asarray(L0_ffn_sw3, dtype=np.float32))
    sw2 = np.ascontiguousarray(np.asarray(L0_ffn_sw2, dtype=np.float32))

    wat = np.ascontiguousarray(
        bv.transpose(0, 3, 2, 1)
          .reshape(NB, NKP, 2, JD, JD)
          .transpose(0, 2, 3, 1, 4)
          .reshape(NB, 128, NKP, JD))

    ident, ltri, iota8, revi, iotar, iotap = _consts()

    nc = build_nc()
    lower_extended_insts(nc)

    in_maps = []
    p = np.arange(128)
    tau, j = p // JD, p % JD
    for core in range(NCORE):
        n = core // 4
        tq = core % 4
        pconst = ((n * 8192 + j).astype(np.uint64) * ROWLEN
                  + DPAD + tq * TQW - (KS - 1) + tau).astype(np.uint32).reshape(128, 1)
        assert ROWLEN == 4096
        gg = np.arange(16)
        aidx = np.zeros((128, 64), np.uint32)
        for tqq in range(4):
            rowbase = (n * 4 + tqq) * (NB * JD) + (core % 4) * TOKC
            aidx[:, 16 * tqq:16 * (tqq + 1)] = ((rowbase + gg[None, :] * 128 + p[:, None])
                                                * TQW).astype(np.uint32)
        cme = np.array([[core, core * TOKC, 0, 0]], dtype=np.int32)
        cc = np.arange(NG)
        tokid = (core * TOKC + cc[None, :] * 128 + p[:, None] + 1).astype(np.float32)
        locpad = (TTOK + cc[None, :] * 128 + p[:, None]).astype(np.float32)
        in_maps.append({
            "x_in": np.ascontiguousarray(x[core * TOKC:(core + 1) * TOKC]),
            "wg_in": wg, "sw1_in": sw1, "sw3_in": sw3, "sw2_in": sw2,
            "we1_in": np.ascontiguousarray(w1[core]),
            "we3_in": np.ascontiguousarray(w3[core]),
            "we2_in": np.ascontiguousarray(w2[core]),
            "wat_in": wat, "cols_in": cols, "rows_in": rows,
            "ident_in": ident, "ltri_in": ltri, "iota8_in": iota8,
            "revi_in": revi, "iotar_in": iotar, "iotap_in": iotap,
            "pconst_in": pconst, "aidx_in": aidx, "cme_in": cme,
            "zpad_in": np.zeros((1, ZW), np.float32),
            "tokid_in": tokid, "locpad_in": locpad,
        })

    res = run_bass_kernel_spmd(nc, in_maps, core_ids=list(range(NCORE)))
    out = np.concatenate([res.results[i]["y_out"] for i in range(NCORE)], axis=0)
    return out.reshape(2, 8192, 2048)


# version 09
# speedup vs baseline: 1.0514x; 1.0514x over previous
from contextlib import ExitStack

import numpy as np
import ml_dtypes

import concourse.bass as bass
import concourse.mybir as mybir
import concourse.library_config as library_config
from concourse.library_overlay import lower_extended_insts
from concourse.bass_utils import run_bass_kernel_spmd

F32 = mybir.dt.float32
BF16 = mybir.dt.bfloat16
I32 = mybir.dt.int32
I16 = mybir.dt.int16
U32 = mybir.dt.uint32
AL = mybir.AluOpType
AF = mybir.ActivationFunctionType
AX = mybir.AxisListType

NCORE = 8
TTOK = 16384
TOKC = 2048
D = 2048
H = 2816
E = 8
CAP = 2048
DPAD = 2048
ROWLEN = DPAD + D
NG = TOKC // 128
NKD = D // 128
NKH = H // 128
NQ = TOKC // 512

NB = 128
JD = 64
KS = 16
NKP = KS // 2
TQW = 512
AW = TQW + KS

NH_K = NKH * NQ
NDC = 8
DCW = 256
MM_PER_PHASE = 2 * NH_K + NDC * 16
HV_PER_PHASE = NH_K + NDC * 16
EV_PER_PHASE = NH_K
YW_PER_PHASE = 16 * 64


def mm_base(which):
    return which * MM_PER_PHASE


def hv_base(which):
    return which * HV_PER_PHASE


def ev_base(which):
    return which * EV_PER_PHASE


def w1_wait(which, hc):
    return 16 * 8 * (which * 11 + hc // 2 + 1)


def w2_wait(which, dc):
    return 16 * 4 * (which * 4 + dc // 2 + 1)


RV_STATS0 = NQ + NG
KV_CUM0 = 2 * NG
KV_PRE0 = KV_CUM0 + 4
KV_TILE = 14
KV_DONE = KV_PRE0 + KV_TILE * NG

ZW = TTOK + TOKC + 128


class Chain:
    def __init__(self, nc, name, ctx, step=1):
        self.sem = ctx.enter_context(nc.semaphore(name))
        self.n = 0
        self.step = step

    def bump(self, instr, k=1):
        self.n += self.step * k
        instr.then_inc(self.sem, self.step * k)
        return self.n

    def wait(self, eng, n=None):
        eng.wait_ge(self.sem, self.n if n is None else n)


def build_nc():
    nc = bass.Bass("TRN2", target_bir_lowering=False, debug=False)

    x_in = nc.dram_tensor("x_in", [TOKC, D], F32, kind="ExternalInput").ap()
    wg_in = nc.dram_tensor("wg_in", [D, E], F32, kind="ExternalInput").ap()
    sw1_in = nc.dram_tensor("sw1_in", [D, H], F32, kind="ExternalInput").ap()
    sw3_in = nc.dram_tensor("sw3_in", [D, H], F32, kind="ExternalInput").ap()
    sw2_in = nc.dram_tensor("sw2_in", [H, D], F32, kind="ExternalInput").ap()
    we1_in = nc.dram_tensor("we1_in", [D, H], F32, kind="ExternalInput").ap()
    we3_in = nc.dram_tensor("we3_in", [D, H], F32, kind="ExternalInput").ap()
    we2_in = nc.dram_tensor("we2_in", [H, D], F32, kind="ExternalInput").ap()
    wat_in = nc.dram_tensor("wat_in", [NB, 128, NKP, JD], F32, kind="ExternalInput").ap()
    cols_in = nc.dram_tensor("cols_in", [1, NB], I32, kind="ExternalInput").ap()
    rows_in = nc.dram_tensor("rows_in", [1, NB], I32, kind="ExternalInput").ap()
    ident_in = nc.dram_tensor("ident_in", [8, 8], F32, kind="ExternalInput").ap()
    ltri_in = nc.dram_tensor("ltri_in", [128, 128], F32, kind="ExternalInput").ap()
    iota8_in = nc.dram_tensor("iota8_in", [128, E], F32, kind="ExternalInput").ap()
    revi_in = nc.dram_tensor("revi_in", [128, E], F32, kind="ExternalInput").ap()
    iotar_in = nc.dram_tensor("iotar_in", [128, 128], F32, kind="ExternalInput").ap()
    iotap_in = nc.dram_tensor("iotap_in", [128, 1], F32, kind="ExternalInput").ap()
    pconst_in = nc.dram_tensor("pconst_in", [128, 1], U32, kind="ExternalInput").ap()
    aidx_in = nc.dram_tensor("aidx_in", [128, 64], U32, kind="ExternalInput").ap()
    cme_in = nc.dram_tensor("cme_in", [1, 4], I32, kind="ExternalInput").ap()
    zpad_in = nc.dram_tensor("zpad_in", [1, ZW], F32, kind="ExternalInput")
    tokid_in = nc.dram_tensor("tokid_in", [128, NG], F32, kind="ExternalInput").ap()
    locpad_in = nc.dram_tensor("locpad_in", [128, NG], F32, kind="ExternalInput").ap()

    y_out = nc.dram_tensor("y_out", [TOKC, D], F32, kind="ExternalOutput").ap()

    ag1_in = nc.dram_tensor("ag1_in", [TOKC, ROWLEN], BF16)
    ag1_out = nc.dram_tensor("ag1_out", [TTOK, ROWLEN], BF16)
    ag2_in = nc.dram_tensor("ag2_in", [1, E], F32)
    ag2_out = nc.dram_tensor("ag2_out", [NCORE, E], F32)
    zsc = nc.dram_tensor("zsc", [1, ZW], F32)
    rbuf = nc.dram_tensor("rbuf", [1, ZW], F32)
    ag4_in = nc.dram_tensor("ag4_in", [CAP, D], BF16)
    ag4_out = nc.dram_tensor("ag4_out", [NCORE * CAP, D], BF16)
    ag5_in = nc.dram_tensor("ag5_in", [NB * JD, TQW], BF16)
    ag5_out = nc.dram_tensor("ag5_out", [NCORE * NB * JD, TQW], BF16)
    ysh_hbm = nc.dram_tensor("ysh_hbm", [TOKC, D], BF16)
    wattb = nc.dram_tensor("wattb", [NB, 128, NKP, JD], BF16)
    oall = nc.dram_tensor("oall", [NB, JD, TQW], BF16)
    slot_hbm = nc.dram_tensor("slot_hbm", [1, TOKC], I16)

    ag1_flat = ag1_out.ap().rearrange("r c -> (r c)").unsqueeze(0)
    ag5_flat = ag5_out.ap().rearrange("r c -> (r c)").unsqueeze(0)

    with ExitStack() as ctx:
        def S(name, shape, dt):
            return ctx.enter_context(nc.sbuf_tensor(name, shape, dt))

        def P(name, shape, dt):
            return ctx.enter_context(nc.psum_tensor(name, shape, dt))

        def CH(name, step=1):
            return Chain(nc, name, ctx, step)

        arena_a = S("arena_a", [128, NKH * TOKC], BF16)
        arena_b = S("arena_b", [128, 32768], BF16)
        af32 = arena_a[:, :].bitcast(F32)
        xs = af32[:, 0:2 * D]
        xcb = af32[:, 2 * D:4 * D]
        sqs = af32[:, 4 * D:5 * D]
        xnst = arena_a[:, 10 * D:10 * D + 2 * ROWLEN]
        lgfm = af32[0:8, 0:TOKC]
        A6 = 32768
        yout_b = arena_a[:, 0:A6]
        xg = arena_a[:, A6:A6 + 3 * AW]
        wab = arena_a[:, A6 + 3 * AW:A6 + 3 * AW + 3 * NKP * JD]
        oev = arena_a[0:64, A6 + 3 * AW + 3 * NKP * JD:A6 + 3 * AW + 3 * NKP * JD + 2 * TQW]
        x7 = af32[:, 0:2 * D]
        acc7 = af32[:, 2 * D:4 * D]
        out7 = af32[:, 4 * D:6 * D]
        att7 = arena_a[:, 12 * D:14 * D]
        ysh7 = arena_a[:, 14 * D:16 * D]

        lnsc = S("lnsc", [128, 16], F32)
        w1sl = S("w1sl", [128, 2 * NKD * 128], BF16)
        w3sl = S("w3sl", [128, 2 * NKD * 128], BF16)
        w2sl = S("w2sl", [128, 2 * NKH * DCW], BF16)
        w2f32 = w2sl[:, :].bitcast(F32)
        idxf = w2f32[0:16, 0:128]
        wgs = S("wgs", [128, NKD * E], BF16)
        scr = S("scr", [128, 2 * 512], F32)
        ost = S("ost", [128, 2 * DCW], BF16)

        ltm = S("ltm", [128, NG * E], F32)
        r8a = S("r8a", [128, E], F32)
        r8b = S("r8b", [128, E], F32)
        eidx_all = S("eidx_all", [128, NG], F32)
        gate_all = S("gate_all", [128, NG], F32)
        oh_all = S("oh_all", [128, NG * E], F32)
        pos_all = S("pos_all", [128, NG], F32)
        gscale = S("gscale", [128, NG], F32)
        slot_i16 = S("slot_i16", [128, NG], I16)
        slot_u32 = S("slot_u32", [128, NG], U32)
        tokid_sb = S("tokid_sb", [128, NG], F32)
        locpad_sb = S("locpad_sb", [128, NG], F32)
        base_sb = S("base_sb", [1, 2 * E], F32)
        ca_sb = S("ca_sb", [NCORE, E], F32)
        selv = S("selv", [NCORE, 1], F32)
        cmeb = S("cmeb", [16, 1], F32)
        cmef = S("cmef", [1, 1], F32)
        thpf = S("thpf", [1, 2 * E], F32)
        thbc = S("thbc", [128, 2 * E], F32)
        tcol = S("tcol", [128, 8], F32)

        ids = S("ids", [8, 8], F32)
        ltri = S("ltri", [128, 128], F32)
        iota8 = S("iota8", [128, E], F32)
        revi = S("revi", [128, E], F32)
        iotar = S("iotar", [128, 128], F32)
        iotap = S("iotap", [128, 1], F32)
        ones1 = S("ones1", [1, 128], F32)
        ones128 = S("ones128", [128, 1], F32)
        cme_sb = S("cme_sb", [1, 4], I32)
        pc_sb = S("pc_sb", [128, 1], U32)
        aidx_sb = S("aidx_sb", [128, 64], U32)

        idx16 = S("idx16", [128, 128], I16)
        sloti = S("sloti", [128, 128], I16)

        colsr = S("colsr", [1, NB], I32)
        rowsr = S("rowsr", [1, NB], I32)
        colf = S("colf", [1, NB], F32)
        rrf = S("rrf", [1, NB], F32)
        colbc = S("colbc", [128, NB], F32)
        colrep = S("colrep", [128, NB], U32)
        rowp = S("rowp", [128, 1], F32)
        smat = S("smat", [128, 128], BF16)

        pa0 = P("pa0", [128, 512], F32)
        pa1 = P("pa1", [128, 512], F32)
        pb0 = P("pb0", [128, 512], F32)
        pb1 = P("pb1", [128, 512], F32)
        py0 = P("py0", [128, 512], F32)
        py1 = P("py1", [128, 512], F32)
        pt0 = P("pt0", [128, 512], F32)
        pt1 = P("pt1", [128, 512], F32)
        pas = [pa0, pa1]
        pbs = [pb0, pb1]
        pyb = [py0, py1]

        c_init = CH("c_init", 16)
        c_dx = [CH("c_dx0", 16), CH("c_dx1", 16)]
        c_lv = CH("c_lv")
        c_ls = CH("c_ls")
        c_xnw = [CH("c_xnw0", 16), CH("c_xnw1", 16)]
        c_cc = CH("c_cc")
        c_fm = CH("c_fm", 16)
        c_pv = CH("c_pv")
        c_pt = CH("c_pt")
        c_pg = CH("c_pg")
        c_sc = CH("c_sc", 16)
        c_gm = CH("c_gm", 16)
        c_wg = CH("c_wg", 16)
        c_wt = CH("c_wt", 16)
        c_sm2 = CH("c_sm2", 16)
        c_rt = CH("c_rt")
        c_rv = CH("c_rv")
        c_rs = CH("c_rs")
        c_ct = CH("c_ct")
        c_cv = CH("c_cv")
        c_kt = CH("c_kt")
        c_kv = CH("c_kv")
        c_w1 = [CH("c_w10", 16), CH("c_w11", 16)]
        c_w2 = [CH("c_w20", 16), CH("c_w21", 16)]
        c_mm = CH("c_mm")
        c_ev = CH("c_ev")
        c_hv = CH("c_hv")
        c_yw = [CH("c_yw0", 16), CH("c_yw1", 16)]
        c_vl = CH("c_vl")
        c_xd = CH("c_xd", 16)
        c_ga = [CH("c_ga0", 16), CH("c_ga1", 16), CH("c_ga2", 16)]
        c_wa = [CH("c_wa0", 16), CH("c_wa1", 16), CH("c_wa2", 16)]
        c_am = CH("c_am")
        c_ae = CH("c_ae")
        c_ao = [CH("c_ao0", 16), CH("c_ao1", 16)]
        c_or = CH("c_or", 16)
        c_sm = CH("c_sm")
        c_sv = CH("c_sv")
        c_a5 = CH("c_a5", 16)
        c_mg = CH("c_mg", 16)
        c_si = CH("c_si", 16)
        c_px = [CH("c_px0", 16), CH("c_px1", 16)]
        c_pys = [CH("c_pys0", 16), CH("c_pys1", 16)]
        c_pat = [CH("c_pat0", 16), CH("c_pat1", 16)]
        c_as = CH("c_as")
        c_out = [CH("c_out0", 16), CH("c_out1", 16)]

        def emit_ffn_tensor(t, which):
            bm = mm_base(which)
            bh = hv_base(which)
            if which == 0:
                c_fm.wait(t, 16 * NKD)
            else:
                c_xd.wait(t, 16 * 4)
            for hc in range(NKH):
                for q in range(NQ):
                    k = hc * 4 + q
                    c_w1[hc % 2].wait(t, w1_wait(which, hc))
                    if k >= 2:
                        c_hv.wait(t, bh + k - 1)
                    pa = pas[q % 2]
                    pb = pbs[q % 2]

                    def rhs(g, q=q):
                        if which == 0:
                            return arena_b[:, g * TOKC + q * 512: g * TOKC + q * 512 + 512]
                        return arena_b[:, q * 8192 + g * 512: q * 8192 + g * 512 + 512]
                    for g in range(NKD):
                        mi = t.matmul(pa[:, :],
                                      w1sl[:, bass.ts(hc % 2, NKD * 128)][:, bass.ts(g, 128)],
                                      rhs(g),
                                      start=(g == 0), stop=(g == NKD - 1))
                    c_mm.bump(mi)
                    for g in range(NKD):
                        mi = t.matmul(pb[:, :],
                                      w3sl[:, bass.ts(hc % 2, NKD * 128)][:, bass.ts(g, 128)],
                                      rhs(g),
                                      start=(g == 0), stop=(g == NKD - 1))
                    c_mm.bump(mi)
            bh2 = bh + NH_K
            for dc in range(NDC):
                for sg in range(16):
                    m = dc * 16 + sg
                    c_w2[dc % 2].wait(t, w2_wait(which, dc))
                    if m >= 2:
                        c_hv.wait(t, bh2 + m - 1)
                    py = pyb[m % 2]
                    for hc in range(NKH):
                        mi = t.matmul(py[:, 0:DCW],
                                      arena_a[:, hc * TOKC + sg * 128: hc * TOKC + sg * 128 + 128],
                                      w2sl[:, bass.ts(dc % 2, NKH * DCW)][:, bass.ts(hc, DCW)],
                                      start=(hc == 0), stop=(hc == NKH - 1))
                    c_mm.bump(mi)

        def emit_ffn_scalar(sc, which):
            bm = mm_base(which)
            bh = hv_base(which)
            for hc in range(NKH):
                for q in range(NQ):
                    k = hc * 4 + q
                    c_mm.wait(sc, bm + 2 * k + 1)
                    if k >= 2:
                        c_hv.wait(sc, bh + k - 1)
                    c_ev.bump(sc.activation(scr[:, bass.ts(k % 2, 512)], pas[q % 2][:, :], AF.Silu))

        def emit_ffn_vector(v, which):
            bm = mm_base(which)
            be = ev_base(which)
            for hc in range(NKH):
                for q in range(NQ):
                    k = hc * 4 + q
                    c_ev.wait(v, be + k + 1)
                    c_mm.wait(v, bm + 2 * k + 2)
                    c_hv.bump(v.tensor_mul(arena_a[:, hc * TOKC + q * 512: hc * TOKC + (q + 1) * 512],
                                           scr[:, bass.ts(k % 2, 512)], pbs[q % 2][:, :]))
            bm2 = bm + 2 * NH_K
            byw = which * YW_PER_PHASE
            for dc in range(NDC):
                for sg in range(16):
                    m = dc * 16 + sg
                    c_mm.wait(v, bm2 + m + 1)
                    if m >= 2 or which == 1:
                        c_yw[m % 2].wait(v, byw + 16 * (m // 2))
                    c_hv.bump(v.tensor_copy(ost[:, bass.ts(m % 2, DCW)], pyb[m % 2][:, 0:DCW]))

        def emit_ffn_sync(s, out_ap, which):
            bh2 = hv_base(which) + NH_K
            for dc in range(NDC):
                for sg in range(16):
                    m = dc * 16 + sg
                    c_hv.wait(s, bh2 + m + 1)
                    c_yw[m % 2].bump(s.dma_start(
                        out=out_ap[bass.ts(sg, 128), bass.ts(dc, DCW)],
                        in_=ost[:, bass.ts(m % 2, DCW)]))

        def emit_ffn_gpsimd_weights(g, w1a, w3a, w2a, which):
            bm = mm_base(which)
            for hc in range(NKH):
                if hc >= 2:
                    c_mm.wait(g, bm + 8 * (hc - 1))
                if which == 0 and hc < 16:
                    c_wt.bump(g.dma_start(out=wattb.ap()[8 * hc:8 * (hc + 1), :, :, :],
                                          in_=wat_in[8 * hc:8 * (hc + 1), :, :, :]))
                for wsl, wa in ((w1sl, w1a), (w3sl, w3a)):
                    for part in range(4):
                        r0 = part * (NKD // 4)
                        c_w1[hc % 2].bump(g.dma_start(
                            out=wsl[:, bass.ts(hc % 2, NKD * 128)]
                                .rearrange("p (gg h) -> p gg h", gg=NKD)[:, r0:r0 + NKD // 4, :],
                            in_=wa[:, 128 * hc:128 * (hc + 1)]
                                .rearrange("(gg p) h -> p gg h", p=128)[:, r0:r0 + NKD // 4, :]))
            for dc in range(NDC):
                if dc >= 2:
                    c_mm.wait(g, bm + 2 * NH_K + 16 * (dc - 1))
                for part in range(4):
                    h0 = part * 6
                    h1 = min(NKH, h0 + 6)
                    c_w2[dc % 2].bump(g.dma_start(
                        out=w2sl[:, bass.ts(dc % 2, NKH * DCW)]
                            .rearrange("p (hh d) -> p hh d", hh=NKH)[:, h0:h1, :],
                        in_=w2a[:, DCW * dc:DCW * (dc + 1)]
                            .rearrange("(hh p) d -> p hh d", p=128)[:, h0:h1, :]))

        block = ctx.enter_context(nc.Block())

        @block.sync
        def _(s):
            for ap_in, ap_sb in [(ident_in, ids), (ltri_in, ltri), (iota8_in, iota8),
                                 (revi_in, revi), (iotar_in, iotar), (iotap_in, iotap),
                                 (cols_in, colsr), (rows_in, rowsr), (cme_in, cme_sb),
                                 (pconst_in, pc_sb), (aidx_in, aidx_sb),
                                 (tokid_in, tokid_sb), (locpad_in, locpad_sb)]:
                c_init.bump(s.dma_start(out=ap_sb[:, :], in_=ap_in[:, :]))
            for g in range(NG):
                if g >= 2:
                    c_lv.wait(s, 6 * (g - 2) + 3)
                c_dx[g % 2].bump(s.dma_start(out=xs[:, bass.ts(g % 2, D)],
                                             in_=x_in[bass.ts(g, 128), :]))
                if g >= 1:
                    gg = g - 1
                    c_lv.wait(s, 6 * gg + 6)
                    c_xnw[gg % 2].bump(s.dma_start(out=ag1_in.ap()[bass.ts(gg, 128), :],
                                                   in_=xnst[:, bass.ts(gg % 2, ROWLEN)]))
            gg = NG - 1
            c_lv.wait(s, 6 * gg + 6)
            c_xnw[gg % 2].bump(s.dma_start(out=ag1_in.ap()[bass.ts(gg, 128), :],
                                           in_=xnst[:, bass.ts(gg % 2, ROWLEN)]))
            s.wait_ge(c_xnw[0].sem, 16 * 8)
            s.wait_ge(c_xnw[1].sem, 16 * 8)
            for g in range(NKD):
                c_fm.bump(s.dma_start_transpose(arena_b[:, bass.ts(g, TOKC)],
                                                ag1_in.ap()[:, DPAD + 128 * g: DPAD + 128 * (g + 1)]))
                s.wait_ge(c_fm.sem, c_fm.n)
            c_cc.wait(s, 1)
            c_sm2.bump(s.dma_start(out=ca_sb[:, :], in_=ag2_out.ap()[:, :]))
            c_kv.wait(s, KV_DONE)
            for gg8 in range(8):
                c_sm2.bump(s.dma_start(
                    out=slot_hbm.ap().rearrange("o (q c g) -> o q c g", q=16, c=16, g=8)[:, :, :, gg8].squeeze(0),
                    in_=slot_i16[16 * gg8:16 * (gg8 + 1), :],
                    allow_slow_non_contiguous=True))
            emit_ffn_sync(s, ysh_hbm.ap(), 0)
            emit_ffn_sync(s, ag4_in.ap(), 1)
            c_wt.wait(s, 16 * 16)
            c_mm.wait(s, 2 * MM_PER_PHASE)
            for b in range(NB + 2):
                if b < NB:
                    buf = b % 3
                    if b >= 3:
                        c_am.wait(s, b - 2)
                    c_wa[buf].bump(s.dma_start(
                        out=wab[:, bass.ts(buf, NKP * JD)],
                        in_=wattb.ap()[b, :, :, :].rearrange("p k i -> p (k i)")))
                if b >= 2:
                    bo = b - 2
                    c_ae.wait(s, bo + 1)
                    c_ao[bo % 2].bump(s.dma_start(out=oall.ap()[bo, :, :],
                                                  in_=oev[:, bass.ts(bo % 2, TQW)]))
            c_cc.wait(s, 5)
            c_sm2.wait(s, 16 * 9)
            c_si.bump(s.dma_start(out=sloti[0:16, :],
                                  in_=slot_hbm.ap().rearrange("o (q c) -> (o q) c", q=16)))
            for g in range(NG):
                if g >= 2:
                    c_as.wait(s, 3 * (g - 2) + 3)
                c_px[g % 2].bump(s.dma_start(out=x7[:, bass.ts(g % 2, D)],
                                             in_=x_in[bass.ts(g, 128), :]))
                if g >= 2:
                    c_as.wait(s, 3 * (g - 2) + 2)
                c_pys[g % 2].bump(s.dma_start(out=ysh7[:, bass.ts(g % 2, D)],
                                              in_=ysh_hbm.ap()[bass.ts(g, 128), :]))
                if g >= 2:
                    go = g - 2
                    c_as.wait(s, 3 * go + 3)
                    c_out[go % 2].bump(s.dma_start(out=y_out[bass.ts(go, 128), :],
                                                   in_=out7[:, bass.ts(go % 2, D)]))
            for go in (NG - 2, NG - 1):
                c_as.wait(s, 3 * go + 3)
                c_out[go % 2].bump(s.dma_start(out=y_out[bass.ts(go, 128), :],
                                               in_=out7[:, bass.ts(go % 2, D)]))
            s.wait_ge(c_out[0].sem, 16 * 8)
            s.wait_ge(c_out[1].sem, 16 * 8)

        @block.vector
        def _(v):
            v.memset(ones1[:, :], 1.0)
            v.memset(ones128[:, :], 1.0)
            v.memset(base_sb[:, :], 0.0)
            for g in range(NG):
                bi = g % 2
                c_dx[bi].wait(v, 16 * (g // 2 + 1))
                xsg = xs[:, bass.ts(bi, D)]
                xcg = xcb[:, bass.ts(bi, D)]
                sl = lnsc[:, 8 * bi: 8 * bi + 8]
                c_lv.bump(v.reduce_sum(sl[:, 0:1], xsg, axis=AX.X, opt_output=False))
                c_lv.wait(v)
                c_lv.bump(v.tensor_scalar(sl[:, 1:2], sl[:, 0:1], 1.0 / D, None, AL.mult))
                c_lv.wait(v)
                if g >= 2:
                    c_ls.wait(v, 2 * (g - 2) + 1)
                c_lv.bump(v.tensor_scalar(xcg, xsg, sl[:, 1:2], None, AL.subtract))
                c_ls.wait(v, 2 * g + 1)
                c_lv.bump(v.tensor_scalar(sl[:, 3:4], sl[:, 2:3], 1.0 / D, 1e-5, AL.mult, AL.add))
                c_ls.wait(v, 2 * g + 2)
                c_lv.bump(v.reciprocal(sl[:, 5:6], sl[:, 4:5]))
                c_lv.wait(v)
                if g < 2:
                    v.memset(xnst[:, bass.ts(bi, ROWLEN)][:, 0:DPAD], 0.0)
                if g >= 2:
                    c_xnw[bi].wait(v, 16 * (g // 2))
                c_lv.bump(v.tensor_scalar(xnst[:, bass.ts(bi, ROWLEN)][:, DPAD:DPAD + D],
                                          xcg, sl[:, 5:6], None, AL.mult))
            c_init.wait(v, c_init.n)
            c_pv.bump(v.tensor_copy(colf[:, :], colsr[:, :]))
            c_pv.bump(v.tensor_copy(rrf[:, :], rowsr[:, :]))
            c_pv.bump(v.tensor_copy(cmef[:, :], cme_sb[0:1, 0:1]))
            c_pt.wait(v, 3)
            c_pv.bump(v.tensor_copy(colbc[:, :], pt0[0:128, 0:NB]))
            c_pv.bump(v.tensor_copy(rowp[:, :], pt1[0:128, 0:1]))
            c_pv.bump(v.tensor_copy(cmeb[:, :], pt1[0:16, 4:5]))
            c_pv.wait(v)
            c_pv.bump(v.tensor_scalar(smat[:, :], iotar[:, :], rowp[:, 0:1], None, AL.is_equal))
            c_pg.bump(v.tensor_copy(colrep[:, :], colbc[:, :]))
            c_pg.wait(v, 1)
            c_pg.bump(v.tensor_scalar(colrep[:, :], colrep[:, :], 18, None,
                                      AL.logical_shift_left))
            c_pg.wait(v, 2)
            c_pg.bump(v.tensor_tensor(colrep[:, :], colrep[:, :],
                                      pc_sb[:, 0:1].broadcast_to([128, NB]), AL.bitwise_or))
            for q in range(NQ):
                c_rt.wait(v, q + 1)
                c_rv.bump(v.tensor_copy(lgfm[0:8, bass.ts(q, 512)], pt0[0:8, 0:512]))
            for c in range(NG):
                c_rt.wait(v, NQ + c + 1)
                c_rv.bump(v.tensor_copy(ltm[:, bass.ts(c, E)], pt1[0:128, 0:8]))
            for c in range(NG):
                lt = ltm[:, bass.ts(c, E)]
                c_rv.bump(v.tensor_max(r8a[:, 0:4], lt[:, 0:4], lt[:, 4:8]))
                c_rv.wait(v)
                c_rv.bump(v.tensor_max(r8a[:, 4:6], r8a[:, 0:2], r8a[:, 2:4]))
                c_rv.wait(v)
                c_rv.bump(v.tensor_max(tcol[:, 0:1], r8a[:, 4:5], r8a[:, 5:6]))
                c_rv.wait(v)
                c_rv.bump(v.tensor_scalar(r8a[:, 0:8], lt, tcol[:, 0:1], None, AL.is_equal))
                c_rv.wait(v)
                c_rv.bump(v.tensor_mul(r8b[:, 0:8], r8a[:, 0:8], revi[:, :]))
                c_rv.wait(v)
                c_rv.bump(v.tensor_max(r8b[:, 0:4], r8b[:, 0:4], r8b[:, 4:8]))
                c_rv.wait(v)
                c_rv.bump(v.tensor_max(r8b[:, 0:2], r8b[:, 0:2], r8b[:, 2:4]))
                c_rv.wait(v)
                c_rv.bump(v.tensor_max(tcol[:, 1:2], r8b[:, 0:1], r8b[:, 1:2]))
                c_rv.wait(v)
                c_rv.bump(v.tensor_scalar(eidx_all[:, c:c + 1], tcol[:, 1:2], -1.0, 8.0, AL.mult, AL.add))
                c_rv.bump(v.tensor_scalar(tcol[:, 2:3], tcol[:, 0:1], -1.0, None, AL.mult))
                c_rv.wait(v)
                c_rs.wait(v, c + 1)
                c_rv.bump(v.reduce_sum(tcol[:, 3:4], r8b[:, 0:8], axis=AX.X, opt_output=False))
                c_rv.wait(v)
                c_kv.bump(v.reciprocal(gate_all[:, c:c + 1], tcol[:, 3:4]))
                c_kv.wait(v)
                c_kv.bump(v.tensor_scalar(oh_all[:, bass.ts(c, E)], iota8[:, :],
                                          eidx_all[:, c:c + 1], None, AL.is_equal))
            for c in range(NG):
                c_ct.wait(v, 2 * c + 1)
                c_cv.bump(v.tensor_copy(tcol[:, 0:8], pt0[0:128, 0:8]))
                c_cv.wait(v)
                c_cv.bump(v.tensor_mul(tcol[:, 0:8], tcol[:, 0:8], oh_all[:, bass.ts(c, E)]))
                c_cv.wait(v)
                c_cv.bump(v.reduce_sum(pos_all[:, c:c + 1], tcol[:, 0:8], axis=AX.X, opt_output=False))
                c_ct.wait(v, 2 * c + 2)
                c_cv.bump(v.tensor_add(base_sb[0:1, 0:E], base_sb[0:1, 0:E], pt1[0:1, 0:8]))
                c_cv.wait(v)
            c_sm2.wait(v, 16)
            c_kv.bump(v.tensor_tensor(selv[:, :], iotap[0:8, :], cmeb[0:8, :], AL.is_lt))
            c_kt.wait(v, 1)
            c_kv.bump(v.tensor_copy(thpf[0:1, 8:16], pt0[0:1, 0:8]))
            c_kv.wait(v)
            c_kv.bump(v.tensor_scalar(thpf[0:1, 0:8], thpf[0:1, 8:16], -1.0, float(CAP), AL.mult, AL.add))
            c_kt.wait(v, 2)
            c_kv.bump(v.tensor_copy(thbc[:, :], pt0[0:128, 0:16]))
            c_kv.wait(v)
            for c in range(NG):
                ohc = oh_all[:, bass.ts(c, E)]
                c_kv.bump(v.tensor_mul(r8a[:, 0:8], thbc[:, 0:8], ohc))
                c_kv.wait(v)
                c_kv.bump(v.reduce_sum(tcol[:, 0:1], r8a[:, 0:8], axis=AX.X, opt_output=False))
                c_kv.wait(v)
                c_kv.bump(v.tensor_mul(r8a[:, 0:8], thbc[:, 8:16], ohc))
                c_kv.wait(v)
                c_kv.bump(v.reduce_sum(tcol[:, 1:2], r8a[:, 0:8], axis=AX.X, opt_output=False))
                c_kv.wait(v)
                c_kv.bump(v.tensor_tensor(tcol[:, 3:4], pos_all[:, c:c + 1], tcol[:, 0:1], AL.is_lt))
                c_kv.wait(v)
                c_kv.bump(v.tensor_mul(gscale[:, c:c + 1], gate_all[:, c:c + 1], tcol[:, 3:4]))
                c_kv.bump(v.scalar_tensor_tensor(tcol[:, 4:5], eidx_all[:, c:c + 1], float(CAP),
                                                 pos_all[:, c:c + 1], AL.mult, AL.add))
                c_kv.wait(v)
                c_kv.bump(v.tensor_add(tcol[:, 4:5], tcol[:, 4:5], tcol[:, 1:2]))
                c_kv.wait(v)
                c_kv.bump(v.tensor_scalar_min(tcol[:, 6:7], tcol[:, 4:5], float(TTOK - 1)))
                c_kv.wait(v)
                c_kv.bump(v.tensor_tensor(tcol[:, 5:6], tcol[:, 6:7], locpad_sb[:, c:c + 1],
                                          AL.subtract))
                c_kv.bump(v.tensor_copy(slot_i16[:, c:c + 1], tcol[:, 6:7]))
                c_kv.wait(v)
                c_kv.bump(v.tensor_mul(tcol[:, 5:6], tcol[:, 5:6], tcol[:, 3:4]))
                c_kv.wait(v)
                c_kv.bump(v.tensor_add(tcol[:, 5:6], tcol[:, 5:6], locpad_sb[:, c:c + 1]))
                c_kv.wait(v)
                c_kv.bump(v.tensor_copy(slot_u32[:, c:c + 1], tcol[:, 5:6]))
                c_kv.wait(v)
            c_gm.wait(v, 16 * 2)
            c_vl.bump(v.tensor_scalar(idxf[:, :], idxf[:, :], -1.0, None, AL.add))
            c_vl.wait(v)
            c_vl.bump(v.tensor_scalar_max(idxf[:, :], idxf[:, :], 0.0))
            c_vl.wait(v)
            c_vl.bump(v.tensor_copy(idx16[0:16, :], idxf[:, :]))
            emit_ffn_vector(v, 0)
            emit_ffn_vector(v, 1)
            for q in range(64):
                c_sm.wait(v, q + 1)
                c_sv.bump(v.tensor_copy(yout_b[:, bass.ts(q, 512)], pyb[q % 2][:, :]))
            c_mg.wait(v, 16 * 4)
            for g in range(NG):
                bi = g % 2
                c_pat[bi].wait(v, 16 * 4 * (g // 2 + 1))
                c_px[bi].wait(v, 16 * (g // 2 + 1))
                c_pys[bi].wait(v, 16 * (g // 2 + 1))
                if g >= 2:
                    c_out[g % 2].wait(v, 16 * (g // 2))
                c_as.bump(v.scalar_tensor_tensor(acc7[:, bass.ts(bi, D)],
                                                 arena_b[:, bass.ts(g, D)], gscale[:, g:g + 1],
                                                 att7[:, bass.ts(bi, D)], AL.mult, AL.add))
                c_as.wait(v)
                c_as.bump(v.tensor_add(acc7[:, bass.ts(bi, D)], acc7[:, bass.ts(bi, D)],
                                       ysh7[:, bass.ts(bi, D)]))
                c_as.wait(v)
                c_as.bump(v.tensor_add(out7[:, bass.ts(bi, D)], acc7[:, bass.ts(bi, D)],
                                       x7[:, bass.ts(bi, D)]))

        @block.scalar
        def _(sc):
            for g in range(NG):
                bi = g % 2
                sl = lnsc[:, 8 * bi: 8 * bi + 8]
                c_lv.wait(sc, 6 * g + 3)
                c_ls.bump(sc.activation(sqs[:, :], xcb[:, bass.ts(bi, D)], AF.Square,
                                        accum_out=sl[:, 2:3]))
                c_lv.wait(sc, 6 * g + 4)
                c_ls.bump(sc.activation(sl[:, 4:5], sl[:, 3:4], AF.Sqrt))
            for c in range(NG):
                c_rv.wait(sc, RV_STATS0 + 11 * c + 10)
                c_rs.bump(sc.activation(r8b[:, 0:8], ltm[:, bass.ts(c, E)], AF.Exp,
                                        bias=tcol[:, 2:3], scale=1.0))
            emit_ffn_scalar(sc, 0)
            emit_ffn_scalar(sc, 1)
            for b in range(NB):
                c_am.wait(sc, b + 1)
                if b >= 2:
                    c_ao[b % 2].wait(sc, 16 * (b // 2))
                c_ae.bump(sc.copy(oev[:, bass.ts(b % 2, TQW)], pas[b % 2][0:64, 0:TQW]))

        @block.tensor
        def _(t):
            c_pv.wait(t, 3)
            c_pt.bump(t.matmul(pt0[0:128, 0:NB], ones1[0:1, :], colf[0:1, :],
                               start=True, stop=True))
            c_pt.bump(t.transpose(pt1[0:128, 0:1], rrf[0:1, 0:NB], ids[0:1, 0:1]))
            c_pt.bump(t.matmul(pt1[0:16, 4:5], ones1[0:1, 0:16], cmef[0:1, 0:1],
                               start=True, stop=True))
            c_pv.wait(t, 6)
            c_fm.wait(t, 16 * NKD)
            c_wg.wait(t, 16 * 2)
            for q in range(NQ):
                if q >= 1:
                    c_rv.wait(t, q)
                for g in range(NKD):
                    mi = t.matmul(pt0[0:8, 0:512], wgs[:, bass.ts(g, E)],
                                  arena_b[:, bass.ts(g, TOKC)][:, bass.ts(q, 512)],
                                  start=(g == 0), stop=(g == NKD - 1))
                c_rt.bump(mi)
            for c in range(NG):
                c_rv.wait(t, NQ + c)
                c_rt.bump(t.transpose(pt1[0:128, 0:8], lgfm[0:8, bass.ts(c, 128)], ids[0:8, 0:8]))
            for c in range(NG):
                c_kv.wait(t, 2 * c + 2)
                if c >= 1:
                    c_cv.wait(t, 4 * (c - 1) + 4)
                mi = t.matmul(pt0[0:128, 0:8], ltri[:, :], oh_all[:, bass.ts(c, E)],
                              start=True, stop=False)
                mi = t.matmul(pt0[0:128, 0:8], ones1[0:1, :], base_sb[0:1, 0:E],
                              start=False, stop=True)
                c_ct.bump(mi)
                mi = t.matmul(pt1[0:1, 0:8], ones128[:, :], oh_all[:, bass.ts(c, E)],
                              start=True, stop=True)
                c_ct.bump(mi)
            c_cv.wait(t, 4 * NG)
            c_kv.wait(t, KV_CUM0 + 1)
            c_kt.bump(t.matmul(pt0[0:1, 0:8], selv[0:8, :], ca_sb[0:8, :], start=True, stop=True))
            c_kv.wait(t, KV_CUM0 + 3)
            c_kt.bump(t.matmul(pt0[0:128, 0:16], ones1[0:1, :], thpf[0:1, 0:16], start=True, stop=True))
            emit_ffn_tensor(t, 0)
            emit_ffn_tensor(t, 1)
            c_pg.wait(t, 3)
            c_ev.wait(t, 2 * EV_PER_PHASE)
            c_hv.wait(t, hv_base(1) + NH_K)
            for b in range(NB):
                buf = b % 3
                c_wa[buf].wait(t, 16 * (b // 3 + 1))
                c_ga[buf].wait(t, 16 * (b // 3 + 1))
                if b >= 2:
                    c_ae.wait(t, b - 1)
                ps = pas[b % 2]
                for kp in range(NKP):
                    mi = t.matmul(ps[0:64, 0:TQW],
                                  wab[:, bass.ts(buf, NKP * JD)][:, bass.ts(kp, JD)],
                                  xg[:, bass.ts(buf, AW)][:, 2 * kp:2 * kp + TQW],
                                  start=(kp == 0), stop=(kp == NKP - 1))
                c_am.bump(mi)
            c_or.wait(t, 16)
            c_pv.wait(t, 7)
            c_hv.wait(t, 2 * HV_PER_PHASE)
            for q in range(64):
                if q >= 2:
                    c_sv.wait(t, q - 1)
                mi = t.matmul(pyb[q % 2][:, :], smat[:, :], arena_b[:, bass.ts(q, 512)],
                              start=True, stop=True)
                c_sm.bump(mi)

        @block.gpsimd
        def _(g):
            c_sc.bump(g.dma_start(out=zsc.ap()[:, :], in_=zpad_in.ap()[:, :]))
            c_wg.bump(g.dma_start(
                out=wgs[:, :].rearrange("p (gg e) -> p gg e", gg=NKD)[:, 0:8, :],
                in_=wg_in.rearrange("(gg p) e -> p gg e", p=128)[:, 0:8, :]))
            c_wg.bump(g.dma_start(
                out=wgs[:, :].rearrange("p (gg e) -> p gg e", gg=NKD)[:, 8:16, :],
                in_=wg_in.rearrange("(gg p) e -> p gg e", p=128)[:, 8:16, :]))
            c_cv.wait(g, 4 * NG)
            c_gm.bump(g.dma_start(out=ag2_in.ap()[:, :], in_=base_sb[0:1, 0:E]))
            g.wait_ge(c_gm.sem, c_gm.n)
            c_cc.bump(g.collective_compute(
                "AllGather", AL.bypass, replica_groups=[list(range(NCORE))],
                ins=[ag2_in.ap().opt()], outs=[ag2_out.ap().opt()]))
            c_kv.wait(g, KV_DONE)
            g.wait_ge(c_sc.sem, 16)
            for c in range(NG):
                c_sc.bump(g.indirect_dma_start(
                    out=zsc.ap(),
                    out_offset=bass.IndirectOffsetOnAxis(ap=slot_u32[:, c:c + 1], axis=1),
                    in_=tokid_sb[:, c:c + 1], in_offset=None))
            g.wait_ge(c_sc.sem, 16 * 17)
            c_cc.bump(g.collective_compute(
                "AllReduce", AL.add, replica_groups=[list(range(NCORE))],
                ins=[zsc.ap().opt()], outs=[rbuf.ap().opt()]))
            c_cc.wait(g, 2)
            with g.register("ro") as ro:
                g.load(ro, cme_sb[0:1, 1:2])
                so = g.snap(ro, min_val=0, max_val=(NCORE - 1) * TOKC)
                c_gm.bump(g.dma_start(
                    out=idxf[:, :],
                    in_=rbuf.ap()[0:1, bass.DynSlice(so, TOKC)]
                        .rearrange("a (c r) -> (a r) c", r=16),
                    allow_slow_non_contiguous=True))
            c_xnw[0].wait(g, 16 * 8)
            c_xnw[1].wait(g, 16 * 8)
            c_cc.bump(g.collective_compute(
                "AllGather", AL.bypass, replica_groups=[list(range(NCORE))],
                ins=[ag1_in.ap().opt()], outs=[ag1_out.ap().opt()]))
            c_vl.wait(g, 3)
            for r in range(1, 8):
                c_gm.bump(g.dma_start(out=idx16[16 * r:16 * (r + 1), :], in_=idx16[0:16, :]))
            g.wait_ge(c_gm.sem, c_gm.n)
            g.load_library(library_config.mlp)
            emit_ffn_gpsimd_weights(g, sw1_in, sw3_in, sw2_in, 0)
            c_mm.wait(g, mm_base(0) + 2 * NH_K)
            c_cc.wait(g, 3)
            for kk in range(4):
                c_xd.bump(g.dma_gather(
                    arena_b[:, 8192 * kk:8192 * (kk + 1)].rearrange("p (a b) -> p a b", b=512),
                    ag1_out.ap()[:, DPAD:DPAD + D],
                    idx16[:, 32 * kk:32 * (kk + 1)],
                    num_idxs=512, num_idxs_reg=512, elem_size=D, elem_step=ROWLEN,
                    transpose=True))
            emit_ffn_gpsimd_weights(g, we1_in, we3_in, we2_in, 1)
            c_yw[0].wait(g, 2 * YW_PER_PHASE)
            c_yw[1].wait(g, 2 * YW_PER_PHASE)
            c_cc.bump(g.collective_compute(
                "AllGather", AL.bypass, replica_groups=[list(range(NCORE))],
                ins=[ag4_in.ap().opt()], outs=[ag4_out.ap().opt()]))
            c_pg.wait(g, 3)
            c_mm.wait(g, 2 * MM_PER_PHASE)
            for b in range(NB):
                buf = b % 3
                if b >= 3:
                    c_am.wait(g, b - 2)
                c_ga[buf].bump(g.indirect_dma_start(
                    out=xg[:, bass.ts(buf, AW)], out_offset=None,
                    in_=ag1_flat,
                    in_offset=bass.IndirectOffsetOnAxis(ap=colrep[:, b:b + 1], axis=1)))
            c_mm.wait(g, mm_base(1) + 2 * NH_K)
            c_ao[0].wait(g, 16 * 64)
            c_ao[1].wait(g, 16 * 64)
            c_or.bump(g.dma_start(out=arena_b[0:128, 0:64 * 512],
                                  in_=oall.ap().rearrange("b i t -> b (i t)")))
            for rs in range(8):
                c_sv.wait(g, 64)
                c_a5.bump(g.dma_start(
                    out=ag5_in.ap().rearrange("(r i) t -> r i t", i=JD)[16 * rs:16 * (rs + 1), :, :],
                    in_=yout_b[16 * rs:16 * (rs + 1), 0:A6].rearrange("r (i t) -> r i t", t=512)))
            g.wait_ge(c_a5.sem, 16 * 8)
            c_cc.bump(g.collective_compute(
                "AllGather", AL.bypass, replica_groups=[list(range(NCORE))],
                ins=[ag5_in.ap().opt()], outs=[ag5_out.ap().opt()]))
            c_si.wait(g, 16)
            for r in range(1, 8):
                c_gm.bump(g.dma_start(out=sloti[16 * r:16 * (r + 1), :], in_=sloti[0:16, :]))
            g.wait_ge(c_gm.sem, c_gm.n)
            c_cc.wait(g, 4)
            c_sm.wait(g, 64)
            for kk in range(4):
                c_mg.bump(g.dma_gather(
                    arena_b[:, :].rearrange("p (a b) -> p a b", a=16)[:, 4 * kk:4 * (kk + 1), :],
                    ag4_out.ap()[:, :],
                    sloti[:, 32 * kk:32 * (kk + 1)],
                    num_idxs=512, num_idxs_reg=512, elem_size=D,
                    transpose=False))
            c_cc.wait(g, 5)
            for gi in range(NG):
                bi = gi % 2
                if gi >= 2:
                    c_as.wait(g, 3 * (gi - 2) + 1)
                for tq in range(4):
                    c_pat[bi].bump(g.indirect_dma_start(
                        out=att7[:, bass.ts(bi, D)][:, bass.ts(tq, 512)], out_offset=None,
                        in_=ag5_flat,
                        in_offset=bass.IndirectOffsetOnAxis(
                            ap=aidx_sb[:, 16 * tq + gi:16 * tq + gi + 1], axis=1)))
            g.wait_ge(c_out[0].sem, 16 * 8)
            g.wait_ge(c_out[1].sem, 16 * 8)

    return nc


def _consts():
    ident = np.eye(8, dtype=np.float32)
    ltri = (np.arange(128)[:, None] < np.arange(128)[None, :]).astype(np.float32)
    iota8 = np.tile(np.arange(E, dtype=np.float32)[None, :], (128, 1))
    revi = 8.0 - iota8
    iotar = np.tile(np.arange(128, dtype=np.float32)[None, :], (128, 1))
    iotap = np.arange(128, dtype=np.float32)[:, None].copy()
    return ident, ltri, iota8, revi, iotar, iotap


def kernel(L0_attn_x, L0_attn_block_values, L0_attn_cols, L0_attn_rows,
           L0_ffn_Wg, L0_ffn_W1, L0_ffn_W3, L0_ffn_W2,
           L0_ffn_sw1, L0_ffn_sw3, L0_ffn_sw2):
    x = np.ascontiguousarray(np.asarray(L0_attn_x, dtype=np.float32)).reshape(TTOK, D)
    bv = np.ascontiguousarray(np.asarray(L0_attn_block_values, dtype=np.float32))
    cols = np.ascontiguousarray(np.asarray(L0_attn_cols, dtype=np.int32).reshape(1, NB))
    rows = np.ascontiguousarray(np.asarray(L0_attn_rows, dtype=np.int32).reshape(1, NB))
    wg = np.ascontiguousarray(np.asarray(L0_ffn_Wg, dtype=np.float32))
    w1 = np.asarray(L0_ffn_W1, dtype=np.float32)
    w3 = np.asarray(L0_ffn_W3, dtype=np.float32)
    w2 = np.asarray(L0_ffn_W2, dtype=np.float32)
    sw1 = np.ascontiguousarray(np.asarray(L0_ffn_sw1, dtype=np.float32))
    sw3 = np.ascontiguousarray(np.asarray(L0_ffn_sw3, dtype=np.float32))
    sw2 = np.ascontiguousarray(np.asarray(L0_ffn_sw2, dtype=np.float32))

    wat = np.ascontiguousarray(
        bv.transpose(0, 3, 2, 1)
          .reshape(NB, NKP, 2, JD, JD)
          .transpose(0, 2, 3, 1, 4)
          .reshape(NB, 128, NKP, JD))

    ident, ltri, iota8, revi, iotar, iotap = _consts()

    nc = build_nc()
    lower_extended_insts(nc)

    in_maps = []
    p = np.arange(128)
    tau, j = p // JD, p % JD
    for core in range(NCORE):
        n = core // 4
        tq = core % 4
        pconst = ((n * 8192 + j).astype(np.uint64) * ROWLEN
                  + DPAD + tq * TQW - (KS - 1) + tau).astype(np.uint32).reshape(128, 1)
        assert ROWLEN == 4096
        gg = np.arange(16)
        aidx = np.zeros((128, 64), np.uint32)
        for tqq in range(4):
            rowbase = (n * 4 + tqq) * (NB * JD) + (core % 4) * TOKC
            aidx[:, 16 * tqq:16 * (tqq + 1)] = ((rowbase + gg[None, :] * 128 + p[:, None])
                                                * TQW).astype(np.uint32)
        cme = np.array([[core, core * TOKC, 0, 0]], dtype=np.int32)
        cc = np.arange(NG)
        tokid = (core * TOKC + cc[None, :] * 128 + p[:, None] + 1).astype(np.float32)
        locpad = (TTOK + cc[None, :] * 128 + p[:, None]).astype(np.float32)
        in_maps.append({
            "x_in": np.ascontiguousarray(x[core * TOKC:(core + 1) * TOKC]),
            "wg_in": wg, "sw1_in": sw1, "sw3_in": sw3, "sw2_in": sw2,
            "we1_in": np.ascontiguousarray(w1[core]),
            "we3_in": np.ascontiguousarray(w3[core]),
            "we2_in": np.ascontiguousarray(w2[core]),
            "wat_in": wat, "cols_in": cols, "rows_in": rows,
            "ident_in": ident, "ltri_in": ltri, "iota8_in": iota8,
            "revi_in": revi, "iotar_in": iotar, "iotap_in": iotap,
            "pconst_in": pconst, "aidx_in": aidx, "cme_in": cme,
            "zpad_in": np.zeros((1, ZW), np.float32),
            "tokid_in": tokid, "locpad_in": locpad,
        })

    res = run_bass_kernel_spmd(nc, in_maps, core_ids=list(range(NCORE)))
    out = np.concatenate([res.results[i]["y_out"] for i in range(NCORE)], axis=0)
    return out.reshape(2, 8192, 2048)


# version 10
# speedup vs baseline: 1.1260x; 1.0710x over previous
from contextlib import ExitStack

import numpy as np
import ml_dtypes

import concourse.bass as bass
import concourse.mybir as mybir
import concourse.library_config as library_config
from concourse.library_overlay import lower_extended_insts
from concourse.bass_utils import run_bass_kernel_spmd

F32 = mybir.dt.float32
BF16 = mybir.dt.bfloat16
I32 = mybir.dt.int32
I16 = mybir.dt.int16
U32 = mybir.dt.uint32
AL = mybir.AluOpType
AF = mybir.ActivationFunctionType
AX = mybir.AxisListType

NCORE = 8
TTOK = 16384
TOKC = 2048
D = 2048
H = 2816
E = 8
CAP = 2048
DPAD = 2048
ROWLEN = DPAD + D
NG = TOKC // 128
NKD = D // 128
NKH = H // 128
NQ = TOKC // 512

NB = 128
JD = 64
KS = 16
NKP = KS // 2
TQW = 512
AW = TQW + KS

NH_K = NKH * NQ
NDC = 8
DCW = 256
MM_PER_PHASE = 2 * NH_K + NDC * 16
HV_PER_PHASE = NH_K + NDC * 16
EV_PER_PHASE = NH_K
YW_PER_PHASE = 16 * 64


def mm_base(which):
    return which * MM_PER_PHASE


def hv_base(which):
    return which * HV_PER_PHASE


def ev_base(which):
    return which * EV_PER_PHASE


def w1_wait(which, hc):
    return 16 * 8 * (which * 11 + hc // 2 + 1)


def w2_wait(which, dc):
    return 16 * 4 * (which * 4 + dc // 2 + 1)


RV_STATS0 = NQ + NG
KV_CUM0 = 2 * NG
KV_PRE0 = KV_CUM0 + 4
KV_TILE = 14
KV_DONE = KV_PRE0 + KV_TILE * NG

ZW = TTOK + TOKC + 128

PKW = 16 + D


class Chain:
    def __init__(self, nc, name, ctx, step=1):
        self.sem = ctx.enter_context(nc.semaphore(name))
        self.n = 0
        self.step = step

    def bump(self, instr, k=1):
        self.n += self.step * k
        instr.then_inc(self.sem, self.step * k)
        return self.n

    def wait(self, eng, n=None):
        eng.wait_ge(self.sem, self.n if n is None else n)


def build_nc():
    nc = bass.Bass("TRN2", target_bir_lowering=False, debug=False)

    x_in = nc.dram_tensor("x_in", [TOKC, D], F32, kind="ExternalInput").ap()
    wg_in = nc.dram_tensor("wg_in", [D, E], F32, kind="ExternalInput").ap()
    sw1_in = nc.dram_tensor("sw1_in", [D, H], F32, kind="ExternalInput").ap()
    sw3_in = nc.dram_tensor("sw3_in", [D, H], F32, kind="ExternalInput").ap()
    sw2_in = nc.dram_tensor("sw2_in", [H, D], F32, kind="ExternalInput").ap()
    we1_in = nc.dram_tensor("we1_in", [D, H], F32, kind="ExternalInput").ap()
    we3_in = nc.dram_tensor("we3_in", [D, H], F32, kind="ExternalInput").ap()
    we2_in = nc.dram_tensor("we2_in", [H, D], F32, kind="ExternalInput").ap()
    wat_in = nc.dram_tensor("wat_in", [NB, 128, NKP, JD], F32, kind="ExternalInput").ap()
    cols_in = nc.dram_tensor("cols_in", [1, NB], I32, kind="ExternalInput").ap()
    rows_in = nc.dram_tensor("rows_in", [1, NB], I32, kind="ExternalInput").ap()
    ident_in = nc.dram_tensor("ident_in", [8, 8], F32, kind="ExternalInput").ap()
    ltri_in = nc.dram_tensor("ltri_in", [128, 128], F32, kind="ExternalInput").ap()
    iota8_in = nc.dram_tensor("iota8_in", [128, E], F32, kind="ExternalInput").ap()
    revi_in = nc.dram_tensor("revi_in", [128, E], F32, kind="ExternalInput").ap()
    iotar_in = nc.dram_tensor("iotar_in", [128, 128], F32, kind="ExternalInput").ap()
    iotap_in = nc.dram_tensor("iotap_in", [128, 1], F32, kind="ExternalInput").ap()
    pconst_in = nc.dram_tensor("pconst_in", [128, 1], U32, kind="ExternalInput").ap()
    aidx_in = nc.dram_tensor("aidx_in", [128, 64], U32, kind="ExternalInput").ap()
    cme_in = nc.dram_tensor("cme_in", [1, 4], I32, kind="ExternalInput").ap()
    zpad_in = nc.dram_tensor("zpad_in", [1, ZW], F32, kind="ExternalInput")
    tokid_in = nc.dram_tensor("tokid_in", [128, NG], F32, kind="ExternalInput").ap()
    locpad_in = nc.dram_tensor("locpad_in", [128, NG], F32, kind="ExternalInput").ap()

    y_out = nc.dram_tensor("y_out", [TOKC, D], F32, kind="ExternalOutput").ap()

    ag1s_in = nc.dram_tensor("ag1s_in", [TOKC, PKW], BF16)
    ag1s_out = nc.dram_tensor("ag1s_out", [TTOK, PKW], BF16)
    ag1_out = nc.dram_tensor("ag1_out", [TTOK, ROWLEN], BF16)
    ag2_in = nc.dram_tensor("ag2_in", [1, E], F32)
    ag2_out = nc.dram_tensor("ag2_out", [NCORE, E], F32)
    zsc = nc.dram_tensor("zsc", [1, ZW], F32)
    rbuf = nc.dram_tensor("rbuf", [1, ZW], F32)
    ag4_in = nc.dram_tensor("ag4_in", [CAP, D], BF16)
    ag4_out = nc.dram_tensor("ag4_out", [NCORE * CAP, D], BF16)
    ag5_in = nc.dram_tensor("ag5_in", [NB * JD, TQW], BF16)
    ag5_out = nc.dram_tensor("ag5_out", [NCORE * NB * JD, TQW], BF16)
    ysh_hbm = nc.dram_tensor("ysh_hbm", [TOKC, D], BF16)
    wattb = nc.dram_tensor("wattb", [NB, 128, NKP, JD], BF16)
    oall = nc.dram_tensor("oall", [NB, JD, TQW], BF16)
    slot_hbm = nc.dram_tensor("slot_hbm", [1, TOKC], I16)

    ag1_flat = ag1_out.ap().rearrange("r c -> (r c)").unsqueeze(0)
    ag5_flat = ag5_out.ap().rearrange("r c -> (r c)").unsqueeze(0)

    with ExitStack() as ctx:
        def S(name, shape, dt):
            return ctx.enter_context(nc.sbuf_tensor(name, shape, dt))

        def P(name, shape, dt):
            return ctx.enter_context(nc.psum_tensor(name, shape, dt))

        def CH(name, step=1):
            return Chain(nc, name, ctx, step)

        arena_a = S("arena_a", [128, NKH * TOKC], BF16)
        arena_b = S("arena_b", [128, 32768], BF16)
        af32 = arena_a[:, :].bitcast(F32)
        xs = af32[:, 0:2 * D]
        xcb = af32[:, 2 * D:4 * D]
        sqs = af32[:, 4 * D:5 * D]
        xnst = arena_a[:, 10 * D:10 * D + 2 * PKW]
        lgfm = af32[0:8, 0:TOKC]
        A6 = 32768
        yout_b = arena_a[:, 0:A6]
        xg = arena_a[:, A6:A6 + 3 * AW]
        wab = arena_a[:, A6 + 3 * AW:A6 + 3 * AW + 3 * NKP * JD]
        oev = arena_a[0:64, A6 + 3 * AW + 3 * NKP * JD:A6 + 3 * AW + 3 * NKP * JD + 2 * TQW]
        x7 = af32[:, 0:2 * D]
        acc7 = af32[:, 2 * D:4 * D]
        out7 = af32[:, 4 * D:6 * D]
        att7 = arena_a[:, 12 * D:14 * D]
        ysh7 = arena_a[:, 14 * D:16 * D]

        lnsc = S("lnsc", [128, 16], F32)
        w1sl = S("w1sl", [128, 2 * NKD * 128], BF16)
        w3sl = S("w3sl", [128, 2 * NKD * 128], BF16)
        w2sl = S("w2sl", [128, 2 * NKH * DCW], BF16)
        w2f32 = w2sl[:, :].bitcast(F32)
        idxf = w2f32[0:16, 0:128]
        wgs = S("wgs", [128, NKD * E], BF16)
        scr = S("scr", [128, 2 * 512], F32)
        ost = S("ost", [128, 2 * DCW], BF16)

        ltm = S("ltm", [128, NG * E], F32)
        r8a = S("r8a", [128, E], F32)
        r8b = S("r8b", [128, E], F32)
        eidx_all = S("eidx_all", [128, NG], F32)
        gate_all = S("gate_all", [128, NG], F32)
        oh_all = S("oh_all", [128, NG * E], F32)
        pos_all = S("pos_all", [128, NG], F32)
        gscale = S("gscale", [128, NG], F32)
        slot_i16 = S("slot_i16", [128, NG], I16)
        slot_u32 = S("slot_u32", [128, NG], U32)
        tokid_sb = S("tokid_sb", [128, NG], F32)
        locpad_sb = S("locpad_sb", [128, NG], F32)
        base_sb = S("base_sb", [1, 2 * E], F32)
        ca_sb = S("ca_sb", [NCORE, E], F32)
        selv = S("selv", [NCORE, 1], F32)
        cmeb = S("cmeb", [16, 1], F32)
        cmef = S("cmef", [1, 1], F32)
        thpf = S("thpf", [1, 2 * E], F32)
        thbc = S("thbc", [128, 2 * E], F32)
        tcol = S("tcol", [128, 8], F32)

        ids = S("ids", [8, 8], F32)
        ltri = S("ltri", [128, 128], F32)
        iota8 = S("iota8", [128, E], F32)
        revi = S("revi", [128, E], F32)
        iotar = S("iotar", [128, 128], F32)
        iotap = S("iotap", [128, 1], F32)
        ones1 = S("ones1", [1, 128], F32)
        ones128 = S("ones128", [128, 1], F32)
        cme_sb = S("cme_sb", [1, 4], I32)
        pc_sb = S("pc_sb", [128, 1], U32)
        aidx_sb = S("aidx_sb", [128, 64], U32)

        idx16 = S("idx16", [128, 128], I16)
        sloti = S("sloti", [128, 128], I16)

        colsr = S("colsr", [1, NB], I32)
        rowsr = S("rowsr", [1, NB], I32)
        colf = S("colf", [1, NB], F32)
        rrf = S("rrf", [1, NB], F32)
        colbc = S("colbc", [128, NB], F32)
        colrep = S("colrep", [128, NB], U32)
        rowp = S("rowp", [128, 1], F32)
        smat = S("smat", [128, 128], BF16)

        pa0 = P("pa0", [128, 512], F32)
        pa1 = P("pa1", [128, 512], F32)
        pb0 = P("pb0", [128, 512], F32)
        pb1 = P("pb1", [128, 512], F32)
        py0 = P("py0", [128, 512], F32)
        py1 = P("py1", [128, 512], F32)
        pt0 = P("pt0", [128, 512], F32)
        pt1 = P("pt1", [128, 512], F32)
        pas = [pa0, pa1]
        pbs = [pb0, pb1]
        pyb = [py0, py1]

        c_init = CH("c_init", 16)
        c_dx = [CH("c_dx0", 16), CH("c_dx1", 16)]
        c_lv = CH("c_lv")
        c_ls = CH("c_ls")
        c_xnw = [CH("c_xnw0", 16), CH("c_xnw1", 16)]
        c_cc = CH("c_cc")
        c_fm = CH("c_fm", 16)
        c_pv = CH("c_pv")
        c_pt = CH("c_pt")
        c_pg = CH("c_pg")
        c_exp = CH("c_exp", 16)
        c_sc = CH("c_sc", 16)
        c_gm = CH("c_gm", 16)
        c_wg = CH("c_wg", 16)
        c_wt = CH("c_wt", 16)
        c_sm2 = CH("c_sm2", 16)
        c_rt = CH("c_rt")
        c_rv = CH("c_rv")
        c_rs = CH("c_rs")
        c_ct = CH("c_ct")
        c_cv = CH("c_cv")
        c_kt = CH("c_kt")
        c_kv = CH("c_kv")
        c_w1 = [CH("c_w10", 16), CH("c_w11", 16)]
        c_w2 = [CH("c_w20", 16), CH("c_w21", 16)]
        c_mm = CH("c_mm")
        c_ev = CH("c_ev")
        c_hv = CH("c_hv")
        c_yw = [CH("c_yw0", 16), CH("c_yw1", 16)]
        c_vl = CH("c_vl")
        c_xd = CH("c_xd", 16)
        c_ga = [CH("c_ga0", 16), CH("c_ga1", 16), CH("c_ga2", 16)]
        c_wa = [CH("c_wa0", 16), CH("c_wa1", 16), CH("c_wa2", 16)]
        c_am = CH("c_am")
        c_ae = CH("c_ae")
        c_ao = [CH("c_ao0", 16), CH("c_ao1", 16)]
        c_or = CH("c_or", 16)
        c_sm = CH("c_sm")
        c_sv = CH("c_sv")
        c_a5 = CH("c_a5", 16)
        c_mg = CH("c_mg", 16)
        c_si = CH("c_si", 16)
        c_px = [CH("c_px0", 16), CH("c_px1", 16)]
        c_pys = [CH("c_pys0", 16), CH("c_pys1", 16)]
        c_pat = [CH("c_pat0", 16), CH("c_pat1", 16)]
        c_as = CH("c_as")
        c_out = [CH("c_out0", 16), CH("c_out1", 16)]

        def emit_ffn_tensor(t, which):
            bm = mm_base(which)
            bh = hv_base(which)
            if which == 0:
                c_fm.wait(t, 16 * NKD)
            else:
                c_xd.wait(t, 16 * 4)
            for hc in range(NKH):
                for q in range(NQ):
                    k = hc * 4 + q
                    c_w1[hc % 2].wait(t, w1_wait(which, hc))
                    if k >= 2:
                        c_hv.wait(t, bh + k - 1)
                    pa = pas[q % 2]
                    pb = pbs[q % 2]

                    def rhs(g, q=q):
                        if which == 0:
                            return arena_b[:, g * TOKC + q * 512: g * TOKC + q * 512 + 512]
                        return arena_b[:, q * 8192 + g * 512: q * 8192 + g * 512 + 512]
                    for g in range(NKD):
                        mi = t.matmul(pa[:, :],
                                      w1sl[:, bass.ts(hc % 2, NKD * 128)][:, bass.ts(g, 128)],
                                      rhs(g),
                                      start=(g == 0), stop=(g == NKD - 1))
                    c_mm.bump(mi)
                    for g in range(NKD):
                        mi = t.matmul(pb[:, :],
                                      w3sl[:, bass.ts(hc % 2, NKD * 128)][:, bass.ts(g, 128)],
                                      rhs(g),
                                      start=(g == 0), stop=(g == NKD - 1))
                    c_mm.bump(mi)
            bh2 = bh + NH_K
            for dc in range(NDC):
                for sg in range(16):
                    m = dc * 16 + sg
                    c_w2[dc % 2].wait(t, w2_wait(which, dc))
                    if m >= 2:
                        c_hv.wait(t, bh2 + m - 1)
                    py = pyb[m % 2]
                    for hc in range(NKH):
                        mi = t.matmul(py[:, 0:DCW],
                                      arena_a[:, hc * TOKC + sg * 128: hc * TOKC + sg * 128 + 128],
                                      w2sl[:, bass.ts(dc % 2, NKH * DCW)][:, bass.ts(hc, DCW)],
                                      start=(hc == 0), stop=(hc == NKH - 1))
                    c_mm.bump(mi)

        def emit_ffn_scalar(sc, which):
            bm = mm_base(which)
            bh = hv_base(which)
            for hc in range(NKH):
                for q in range(NQ):
                    k = hc * 4 + q
                    c_mm.wait(sc, bm + 2 * k + 1)
                    if k >= 2:
                        c_hv.wait(sc, bh + k - 1)
                    c_ev.bump(sc.activation(scr[:, bass.ts(k % 2, 512)], pas[q % 2][:, :], AF.Silu))

        def emit_ffn_vector(v, which):
            bm = mm_base(which)
            be = ev_base(which)
            for hc in range(NKH):
                for q in range(NQ):
                    k = hc * 4 + q
                    c_ev.wait(v, be + k + 1)
                    c_mm.wait(v, bm + 2 * k + 2)
                    c_hv.bump(v.tensor_mul(arena_a[:, hc * TOKC + q * 512: hc * TOKC + (q + 1) * 512],
                                           scr[:, bass.ts(k % 2, 512)], pbs[q % 2][:, :]))
            bm2 = bm + 2 * NH_K
            byw = which * YW_PER_PHASE
            for dc in range(NDC):
                for sg in range(16):
                    m = dc * 16 + sg
                    c_mm.wait(v, bm2 + m + 1)
                    if m >= 2 or which == 1:
                        c_yw[m % 2].wait(v, byw + 16 * (m // 2))
                    c_hv.bump(v.tensor_copy(ost[:, bass.ts(m % 2, DCW)], pyb[m % 2][:, 0:DCW]))

        def emit_ffn_sync(s, out_ap, which):
            bh2 = hv_base(which) + NH_K
            for dc in range(NDC):
                for sg in range(16):
                    m = dc * 16 + sg
                    c_hv.wait(s, bh2 + m + 1)
                    c_yw[m % 2].bump(s.dma_start(
                        out=out_ap[bass.ts(sg, 128), bass.ts(dc, DCW)],
                        in_=ost[:, bass.ts(m % 2, DCW)]))

        def emit_ffn_gpsimd_weights(g, w1a, w3a, w2a, which, first_hc=0):
            bm = mm_base(which)
            for hc in range(first_hc, NKH):
                if hc >= 2:
                    c_mm.wait(g, bm + 8 * (hc - 1))
                if which == 0 and 2 <= hc < 18:
                    cch = hc - 2
                    c_wt.bump(g.dma_start(out=wattb.ap()[8 * cch:8 * (cch + 1), :, :, :],
                                          in_=wat_in[8 * cch:8 * (cch + 1), :, :, :]))
                for wsl, wa in ((w1sl, w1a), (w3sl, w3a)):
                    for part in range(4):
                        r0 = part * (NKD // 4)
                        c_w1[hc % 2].bump(g.dma_start(
                            out=wsl[:, bass.ts(hc % 2, NKD * 128)]
                                .rearrange("p (gg h) -> p gg h", gg=NKD)[:, r0:r0 + NKD // 4, :],
                            in_=wa[:, 128 * hc:128 * (hc + 1)]
                                .rearrange("(gg p) h -> p gg h", p=128)[:, r0:r0 + NKD // 4, :]))
            for dc in range(NDC):
                if dc >= 2:
                    c_mm.wait(g, bm + 2 * NH_K + 16 * (dc - 1))
                for part in range(4):
                    h0 = part * 6
                    h1 = min(NKH, h0 + 6)
                    c_w2[dc % 2].bump(g.dma_start(
                        out=w2sl[:, bass.ts(dc % 2, NKH * DCW)]
                            .rearrange("p (hh d) -> p hh d", hh=NKH)[:, h0:h1, :],
                        in_=w2a[:, DCW * dc:DCW * (dc + 1)]
                            .rearrange("(hh p) d -> p hh d", p=128)[:, h0:h1, :]))

        block = ctx.enter_context(nc.Block())

        @block.sync
        def _(s):
            for ap_in, ap_sb in [(ident_in, ids), (ltri_in, ltri), (iota8_in, iota8),
                                 (revi_in, revi), (iotar_in, iotar), (iotap_in, iotap),
                                 (cols_in, colsr), (rows_in, rowsr), (cme_in, cme_sb),
                                 (pconst_in, pc_sb), (aidx_in, aidx_sb),
                                 (tokid_in, tokid_sb), (locpad_in, locpad_sb)]:
                c_init.bump(s.dma_start(out=ap_sb[:, :], in_=ap_in[:, :]))
            for g in range(NG):
                if g >= 2:
                    c_lv.wait(s, 6 * (g - 2) + 3)
                c_dx[g % 2].bump(s.dma_start(out=xs[:, bass.ts(g % 2, D)],
                                             in_=x_in[bass.ts(g, 128), :]))
                if g >= 1:
                    gg = g - 1
                    c_lv.wait(s, 6 * gg + 6)
                    c_xnw[gg % 2].bump(s.dma_start(out=ag1s_in.ap()[bass.ts(gg, 128), :],
                                                   in_=xnst[:, bass.ts(gg % 2, PKW)]))
            gg = NG - 1
            c_lv.wait(s, 6 * gg + 6)
            c_xnw[gg % 2].bump(s.dma_start(out=ag1s_in.ap()[bass.ts(gg, 128), :],
                                           in_=xnst[:, bass.ts(gg % 2, PKW)]))
            s.wait_ge(c_xnw[0].sem, 16 * 8)
            s.wait_ge(c_xnw[1].sem, 16 * 8)
            for g in range(NKD):
                c_fm.bump(s.dma_start_transpose(arena_b[:, bass.ts(g, TOKC)],
                                                ag1s_in.ap()[:, 16 + 128 * g: 16 + 128 * (g + 1)]))
                s.wait_ge(c_fm.sem, c_fm.n)
            c_cc.wait(s, 1)
            c_sm2.bump(s.dma_start(out=ca_sb[:, :], in_=ag2_out.ap()[:, :]))
            c_kv.wait(s, KV_DONE)
            for gg8 in range(8):
                c_sm2.bump(s.dma_start(
                    out=slot_hbm.ap().rearrange("o (q c g) -> o q c g", q=16, c=16, g=8)[:, :, :, gg8].squeeze(0),
                    in_=slot_i16[16 * gg8:16 * (gg8 + 1), :],
                    allow_slow_non_contiguous=True))
            c_cc.wait(s, 3)
            for r8 in range(8):
                c_exp.bump(s.dma_start(
                    out=ag1_out.ap()[2048 * r8:2048 * (r8 + 1), DPAD - 16:ROWLEN],
                    in_=ag1s_out.ap()[2048 * r8:2048 * (r8 + 1), :]))
            emit_ffn_sync(s, ysh_hbm.ap(), 0)
            emit_ffn_sync(s, ag4_in.ap(), 1)
            c_wt.wait(s, 16 * 16)
            c_mm.wait(s, 2 * MM_PER_PHASE)
            for b in range(NB + 2):
                if b < NB:
                    buf = b % 3
                    if b >= 3:
                        c_am.wait(s, b - 2)
                    c_wa[buf].bump(s.dma_start(
                        out=wab[:, bass.ts(buf, NKP * JD)],
                        in_=wattb.ap()[b, :, :, :].rearrange("p k i -> p (k i)")))
                if b >= 2:
                    bo = b - 2
                    c_ae.wait(s, bo + 1)
                    c_ao[bo % 2].bump(s.dma_start(out=oall.ap()[bo, :, :],
                                                  in_=oev[:, bass.ts(bo % 2, TQW)]))
            c_cc.wait(s, 5)
            c_sm2.wait(s, 16 * 9)
            c_si.bump(s.dma_start(out=sloti[0:16, :],
                                  in_=slot_hbm.ap().rearrange("o (q c) -> (o q) c", q=16)))
            for g in range(NG):
                if g >= 2:
                    c_as.wait(s, 3 * (g - 2) + 3)
                c_px[g % 2].bump(s.dma_start(out=x7[:, bass.ts(g % 2, D)],
                                             in_=x_in[bass.ts(g, 128), :]))
                if g >= 2:
                    c_as.wait(s, 3 * (g - 2) + 2)
                c_pys[g % 2].bump(s.dma_start(out=ysh7[:, bass.ts(g % 2, D)],
                                              in_=ysh_hbm.ap()[bass.ts(g, 128), :]))
                if g >= 2:
                    go = g - 2
                    c_as.wait(s, 3 * go + 3)
                    c_out[go % 2].bump(s.dma_start(out=y_out[bass.ts(go, 128), :],
                                                   in_=out7[:, bass.ts(go % 2, D)]))
            for go in (NG - 2, NG - 1):
                c_as.wait(s, 3 * go + 3)
                c_out[go % 2].bump(s.dma_start(out=y_out[bass.ts(go, 128), :],
                                               in_=out7[:, bass.ts(go % 2, D)]))
            s.wait_ge(c_out[0].sem, 16 * 8)
            s.wait_ge(c_out[1].sem, 16 * 8)

        @block.vector
        def _(v):
            v.memset(ones1[:, :], 1.0)
            v.memset(ones128[:, :], 1.0)
            v.memset(base_sb[:, :], 0.0)
            for g in range(NG):
                bi = g % 2
                c_dx[bi].wait(v, 16 * (g // 2 + 1))
                xsg = xs[:, bass.ts(bi, D)]
                xcg = xcb[:, bass.ts(bi, D)]
                sl = lnsc[:, 8 * bi: 8 * bi + 8]
                c_lv.bump(v.reduce_sum(sl[:, 0:1], xsg, axis=AX.X, opt_output=False))
                c_lv.wait(v)
                c_lv.bump(v.tensor_scalar(sl[:, 1:2], sl[:, 0:1], 1.0 / D, None, AL.mult))
                c_lv.wait(v)
                if g >= 2:
                    c_ls.wait(v, 2 * (g - 2) + 1)
                c_lv.bump(v.tensor_scalar(xcg, xsg, sl[:, 1:2], None, AL.subtract))
                c_ls.wait(v, 2 * g + 1)
                c_lv.bump(v.tensor_scalar(sl[:, 3:4], sl[:, 2:3], 1.0 / D, 1e-5, AL.mult, AL.add))
                c_ls.wait(v, 2 * g + 2)
                c_lv.bump(v.reciprocal(sl[:, 5:6], sl[:, 4:5]))
                c_lv.wait(v)
                if g < 2:
                    v.memset(xnst[:, bass.ts(bi, PKW)][:, 0:16], 0.0)
                if g >= 2:
                    c_xnw[bi].wait(v, 16 * (g // 2))
                c_lv.bump(v.tensor_scalar(xnst[:, bass.ts(bi, PKW)][:, 16:PKW],
                                          xcg, sl[:, 5:6], None, AL.mult))
            c_init.wait(v, c_init.n)
            c_pv.bump(v.tensor_copy(colf[:, :], colsr[:, :]))
            c_pv.bump(v.tensor_copy(rrf[:, :], rowsr[:, :]))
            c_pv.bump(v.tensor_copy(cmef[:, :], cme_sb[0:1, 0:1]))
            c_pt.wait(v, 3)
            c_pv.bump(v.tensor_copy(colbc[:, :], pt0[0:128, 0:NB]))
            c_pv.bump(v.tensor_copy(rowp[:, :], pt1[0:128, 0:1]))
            c_pv.bump(v.tensor_copy(cmeb[:, :], pt1[0:16, 4:5]))
            c_pv.wait(v)
            c_pv.bump(v.tensor_scalar(smat[:, :], iotar[:, :], rowp[:, 0:1], None, AL.is_equal))
            c_pg.bump(v.tensor_copy(colrep[:, :], colbc[:, :]))
            c_pg.wait(v, 1)
            c_pg.bump(v.tensor_scalar(colrep[:, :], colrep[:, :], 18, None,
                                      AL.logical_shift_left))
            c_pg.wait(v, 2)
            c_pg.bump(v.tensor_tensor(colrep[:, :], colrep[:, :],
                                      pc_sb[:, 0:1].broadcast_to([128, NB]), AL.bitwise_or))
            for q in range(NQ):
                c_rt.wait(v, q + 1)
                c_rv.bump(v.tensor_copy(lgfm[0:8, bass.ts(q, 512)], pt0[0:8, 0:512]))
            for c in range(NG):
                c_rt.wait(v, NQ + c + 1)
                c_rv.bump(v.tensor_copy(ltm[:, bass.ts(c, E)], pt1[0:128, 0:8]))
            for c in range(NG):
                lt = ltm[:, bass.ts(c, E)]
                c_rv.bump(v.tensor_max(r8a[:, 0:4], lt[:, 0:4], lt[:, 4:8]))
                c_rv.wait(v)
                c_rv.bump(v.tensor_max(r8a[:, 4:6], r8a[:, 0:2], r8a[:, 2:4]))
                c_rv.wait(v)
                c_rv.bump(v.tensor_max(tcol[:, 0:1], r8a[:, 4:5], r8a[:, 5:6]))
                c_rv.wait(v)
                c_rv.bump(v.tensor_scalar(r8a[:, 0:8], lt, tcol[:, 0:1], None, AL.is_equal))
                c_rv.wait(v)
                c_rv.bump(v.tensor_mul(r8b[:, 0:8], r8a[:, 0:8], revi[:, :]))
                c_rv.wait(v)
                c_rv.bump(v.tensor_max(r8b[:, 0:4], r8b[:, 0:4], r8b[:, 4:8]))
                c_rv.wait(v)
                c_rv.bump(v.tensor_max(r8b[:, 0:2], r8b[:, 0:2], r8b[:, 2:4]))
                c_rv.wait(v)
                c_rv.bump(v.tensor_max(tcol[:, 1:2], r8b[:, 0:1], r8b[:, 1:2]))
                c_rv.wait(v)
                c_rv.bump(v.tensor_scalar(eidx_all[:, c:c + 1], tcol[:, 1:2], -1.0, 8.0, AL.mult, AL.add))
                c_rv.bump(v.tensor_scalar(tcol[:, 2:3], tcol[:, 0:1], -1.0, None, AL.mult))
                c_rv.wait(v)
                c_rs.wait(v, c + 1)
                c_rv.bump(v.reduce_sum(tcol[:, 3:4], r8b[:, 0:8], axis=AX.X, opt_output=False))
                c_rv.wait(v)
                c_kv.bump(v.reciprocal(gate_all[:, c:c + 1], tcol[:, 3:4]))
                c_kv.wait(v)
                c_kv.bump(v.tensor_scalar(oh_all[:, bass.ts(c, E)], iota8[:, :],
                                          eidx_all[:, c:c + 1], None, AL.is_equal))
            for c in range(NG):
                c_ct.wait(v, 2 * c + 1)
                c_cv.bump(v.tensor_copy(tcol[:, 0:8], pt0[0:128, 0:8]))
                c_cv.wait(v)
                c_cv.bump(v.tensor_mul(tcol[:, 0:8], tcol[:, 0:8], oh_all[:, bass.ts(c, E)]))
                c_cv.wait(v)
                c_cv.bump(v.reduce_sum(pos_all[:, c:c + 1], tcol[:, 0:8], axis=AX.X, opt_output=False))
                c_ct.wait(v, 2 * c + 2)
                c_cv.bump(v.tensor_add(base_sb[0:1, 0:E], base_sb[0:1, 0:E], pt1[0:1, 0:8]))
                c_cv.wait(v)
            c_sm2.wait(v, 16)
            c_kv.bump(v.tensor_tensor(selv[:, :], iotap[0:8, :], cmeb[0:8, :], AL.is_lt))
            c_kt.wait(v, 1)
            c_kv.bump(v.tensor_copy(thpf[0:1, 8:16], pt0[0:1, 0:8]))
            c_kv.wait(v)
            c_kv.bump(v.tensor_scalar(thpf[0:1, 0:8], thpf[0:1, 8:16], -1.0, float(CAP), AL.mult, AL.add))
            c_kt.wait(v, 2)
            c_kv.bump(v.tensor_copy(thbc[:, :], pt0[0:128, 0:16]))
            c_kv.wait(v)
            for c in range(NG):
                ohc = oh_all[:, bass.ts(c, E)]
                c_kv.bump(v.tensor_mul(r8a[:, 0:8], thbc[:, 0:8], ohc))
                c_kv.wait(v)
                c_kv.bump(v.reduce_sum(tcol[:, 0:1], r8a[:, 0:8], axis=AX.X, opt_output=False))
                c_kv.wait(v)
                c_kv.bump(v.tensor_mul(r8a[:, 0:8], thbc[:, 8:16], ohc))
                c_kv.wait(v)
                c_kv.bump(v.reduce_sum(tcol[:, 1:2], r8a[:, 0:8], axis=AX.X, opt_output=False))
                c_kv.wait(v)
                c_kv.bump(v.tensor_tensor(tcol[:, 3:4], pos_all[:, c:c + 1], tcol[:, 0:1], AL.is_lt))
                c_kv.wait(v)
                c_kv.bump(v.tensor_mul(gscale[:, c:c + 1], gate_all[:, c:c + 1], tcol[:, 3:4]))
                c_kv.bump(v.scalar_tensor_tensor(tcol[:, 4:5], eidx_all[:, c:c + 1], float(CAP),
                                                 pos_all[:, c:c + 1], AL.mult, AL.add))
                c_kv.wait(v)
                c_kv.bump(v.tensor_add(tcol[:, 4:5], tcol[:, 4:5], tcol[:, 1:2]))
                c_kv.wait(v)
                c_kv.bump(v.tensor_scalar_min(tcol[:, 6:7], tcol[:, 4:5], float(TTOK - 1)))
                c_kv.wait(v)
                c_kv.bump(v.tensor_tensor(tcol[:, 5:6], tcol[:, 6:7], locpad_sb[:, c:c + 1],
                                          AL.subtract))
                c_kv.bump(v.tensor_copy(slot_i16[:, c:c + 1], tcol[:, 6:7]))
                c_kv.wait(v)
                c_kv.bump(v.tensor_mul(tcol[:, 5:6], tcol[:, 5:6], tcol[:, 3:4]))
                c_kv.wait(v)
                c_kv.bump(v.tensor_add(tcol[:, 5:6], tcol[:, 5:6], locpad_sb[:, c:c + 1]))
                c_kv.wait(v)
                c_kv.bump(v.tensor_copy(slot_u32[:, c:c + 1], tcol[:, 5:6]))
                c_kv.wait(v)
            c_gm.wait(v, 16 * 2)
            c_vl.bump(v.tensor_scalar(idxf[:, :], idxf[:, :], -1.0, None, AL.add))
            c_vl.wait(v)
            c_vl.bump(v.tensor_scalar_max(idxf[:, :], idxf[:, :], 0.0))
            c_vl.wait(v)
            c_vl.bump(v.tensor_copy(idx16[0:16, :], idxf[:, :]))
            emit_ffn_vector(v, 0)
            emit_ffn_vector(v, 1)
            for q in range(64):
                c_sm.wait(v, q + 1)
                c_sv.bump(v.tensor_copy(yout_b[:, bass.ts(q, 512)], pyb[q % 2][:, :]))
            c_mg.wait(v, 16 * 4)
            for g in range(NG):
                bi = g % 2
                c_pat[bi].wait(v, 16 * 4 * (g // 2 + 1))
                c_px[bi].wait(v, 16 * (g // 2 + 1))
                c_pys[bi].wait(v, 16 * (g // 2 + 1))
                if g >= 2:
                    c_out[g % 2].wait(v, 16 * (g // 2))
                c_as.bump(v.scalar_tensor_tensor(acc7[:, bass.ts(bi, D)],
                                                 arena_b[:, bass.ts(g, D)], gscale[:, g:g + 1],
                                                 att7[:, bass.ts(bi, D)], AL.mult, AL.add))
                c_as.wait(v)
                c_as.bump(v.tensor_add(acc7[:, bass.ts(bi, D)], acc7[:, bass.ts(bi, D)],
                                       ysh7[:, bass.ts(bi, D)]))
                c_as.wait(v)
                c_as.bump(v.tensor_add(out7[:, bass.ts(bi, D)], acc7[:, bass.ts(bi, D)],
                                       x7[:, bass.ts(bi, D)]))

        @block.scalar
        def _(sc):
            for g in range(NG):
                bi = g % 2
                sl = lnsc[:, 8 * bi: 8 * bi + 8]
                c_lv.wait(sc, 6 * g + 3)
                c_ls.bump(sc.activation(sqs[:, :], xcb[:, bass.ts(bi, D)], AF.Square,
                                        accum_out=sl[:, 2:3]))
                c_lv.wait(sc, 6 * g + 4)
                c_ls.bump(sc.activation(sl[:, 4:5], sl[:, 3:4], AF.Sqrt))
            for c in range(NG):
                c_rv.wait(sc, RV_STATS0 + 11 * c + 10)
                c_rs.bump(sc.activation(r8b[:, 0:8], ltm[:, bass.ts(c, E)], AF.Exp,
                                        bias=tcol[:, 2:3], scale=1.0))
            emit_ffn_scalar(sc, 0)
            emit_ffn_scalar(sc, 1)
            for b in range(NB):
                c_am.wait(sc, b + 1)
                if b >= 2:
                    c_ao[b % 2].wait(sc, 16 * (b // 2))
                c_ae.bump(sc.copy(oev[:, bass.ts(b % 2, TQW)], pas[b % 2][0:64, 0:TQW]))

        @block.tensor
        def _(t):
            c_pv.wait(t, 3)
            c_pt.bump(t.matmul(pt0[0:128, 0:NB], ones1[0:1, :], colf[0:1, :],
                               start=True, stop=True))
            c_pt.bump(t.transpose(pt1[0:128, 0:1], rrf[0:1, 0:NB], ids[0:1, 0:1]))
            c_pt.bump(t.matmul(pt1[0:16, 4:5], ones1[0:1, 0:16], cmef[0:1, 0:1],
                               start=True, stop=True))
            c_pv.wait(t, 6)
            c_fm.wait(t, 16 * NKD)
            c_wg.wait(t, 16 * 2)
            for q in range(NQ):
                if q >= 1:
                    c_rv.wait(t, q)
                for g in range(NKD):
                    mi = t.matmul(pt0[0:8, 0:512], wgs[:, bass.ts(g, E)],
                                  arena_b[:, bass.ts(g, TOKC)][:, bass.ts(q, 512)],
                                  start=(g == 0), stop=(g == NKD - 1))
                c_rt.bump(mi)
            for c in range(NG):
                c_rv.wait(t, NQ + c)
                c_rt.bump(t.transpose(pt1[0:128, 0:8], lgfm[0:8, bass.ts(c, 128)], ids[0:8, 0:8]))
            for c in range(NG):
                c_kv.wait(t, 2 * c + 2)
                if c >= 1:
                    c_cv.wait(t, 4 * (c - 1) + 4)
                mi = t.matmul(pt0[0:128, 0:8], ltri[:, :], oh_all[:, bass.ts(c, E)],
                              start=True, stop=False)
                mi = t.matmul(pt0[0:128, 0:8], ones1[0:1, :], base_sb[0:1, 0:E],
                              start=False, stop=True)
                c_ct.bump(mi)
                mi = t.matmul(pt1[0:1, 0:8], ones128[:, :], oh_all[:, bass.ts(c, E)],
                              start=True, stop=True)
                c_ct.bump(mi)
            c_cv.wait(t, 4 * NG)
            c_kv.wait(t, KV_CUM0 + 1)
            c_kt.bump(t.matmul(pt0[0:1, 0:8], selv[0:8, :], ca_sb[0:8, :], start=True, stop=True))
            c_kv.wait(t, KV_CUM0 + 3)
            c_kt.bump(t.matmul(pt0[0:128, 0:16], ones1[0:1, :], thpf[0:1, 0:16], start=True, stop=True))
            emit_ffn_tensor(t, 0)
            emit_ffn_tensor(t, 1)
            c_pg.wait(t, 3)
            c_ev.wait(t, 2 * EV_PER_PHASE)
            c_hv.wait(t, hv_base(1) + NH_K)
            for b in range(NB):
                buf = b % 3
                c_wa[buf].wait(t, 16 * (b // 3 + 1))
                c_ga[buf].wait(t, 16 * (b // 3 + 1))
                if b >= 2:
                    c_ae.wait(t, b - 1)
                ps = pas[b % 2]
                for kp in range(NKP):
                    mi = t.matmul(ps[0:64, 0:TQW],
                                  wab[:, bass.ts(buf, NKP * JD)][:, bass.ts(kp, JD)],
                                  xg[:, bass.ts(buf, AW)][:, 2 * kp:2 * kp + TQW],
                                  start=(kp == 0), stop=(kp == NKP - 1))
                c_am.bump(mi)
            c_or.wait(t, 16)
            c_pv.wait(t, 7)
            c_hv.wait(t, 2 * HV_PER_PHASE)
            for q in range(64):
                if q >= 2:
                    c_sv.wait(t, q - 1)
                mi = t.matmul(pyb[q % 2][:, :], smat[:, :], arena_b[:, bass.ts(q, 512)],
                              start=True, stop=True)
                c_sm.bump(mi)

        @block.gpsimd
        def _(g):
            c_sc.bump(g.dma_start(out=zsc.ap()[:, :], in_=zpad_in.ap()[:, :]))
            for hc in (0, 1):
                for wsl, wa in ((w1sl, sw1_in), (w3sl, sw3_in)):
                    for part in range(4):
                        r0 = part * (NKD // 4)
                        c_w1[hc % 2].bump(g.dma_start(
                            out=wsl[:, bass.ts(hc % 2, NKD * 128)]
                                .rearrange("p (gg h) -> p gg h", gg=NKD)[:, r0:r0 + NKD // 4, :],
                            in_=wa[:, 128 * hc:128 * (hc + 1)]
                                .rearrange("(gg p) h -> p gg h", p=128)[:, r0:r0 + NKD // 4, :]))
            c_wg.bump(g.dma_start(
                out=wgs[:, :].rearrange("p (gg e) -> p gg e", gg=NKD)[:, 0:8, :],
                in_=wg_in.rearrange("(gg p) e -> p gg e", p=128)[:, 0:8, :]))
            c_wg.bump(g.dma_start(
                out=wgs[:, :].rearrange("p (gg e) -> p gg e", gg=NKD)[:, 8:16, :],
                in_=wg_in.rearrange("(gg p) e -> p gg e", p=128)[:, 8:16, :]))
            c_cv.wait(g, 4 * NG)
            c_gm.bump(g.dma_start(out=ag2_in.ap()[:, :], in_=base_sb[0:1, 0:E]))
            g.wait_ge(c_gm.sem, c_gm.n)
            c_cc.bump(g.collective_compute(
                "AllGather", AL.bypass, replica_groups=[list(range(NCORE))],
                ins=[ag2_in.ap().opt()], outs=[ag2_out.ap().opt()]))
            c_kv.wait(g, KV_DONE)
            g.wait_ge(c_sc.sem, 16)
            for c in range(NG):
                c_sc.bump(g.indirect_dma_start(
                    out=zsc.ap(),
                    out_offset=bass.IndirectOffsetOnAxis(ap=slot_u32[:, c:c + 1], axis=1),
                    in_=tokid_sb[:, c:c + 1], in_offset=None))
            g.wait_ge(c_sc.sem, 16 * 17)
            c_cc.bump(g.collective_compute(
                "AllReduce", AL.add, replica_groups=[list(range(NCORE))],
                ins=[zsc.ap().opt()], outs=[rbuf.ap().opt()]))
            c_cc.wait(g, 2)
            with g.register("ro") as ro:
                g.load(ro, cme_sb[0:1, 1:2])
                so = g.snap(ro, min_val=0, max_val=(NCORE - 1) * TOKC)
                c_gm.bump(g.dma_start(
                    out=idxf[:, :],
                    in_=rbuf.ap()[0:1, bass.DynSlice(so, TOKC)]
                        .rearrange("a (c r) -> (a r) c", r=16),
                    allow_slow_non_contiguous=True))
            c_xnw[0].wait(g, 16 * 8)
            c_xnw[1].wait(g, 16 * 8)
            c_cc.bump(g.collective_compute(
                "AllGather", AL.bypass, replica_groups=[list(range(NCORE))],
                ins=[ag1s_in.ap().opt()], outs=[ag1s_out.ap().opt()]))
            c_vl.wait(g, 3)
            for r in range(1, 8):
                c_gm.bump(g.dma_start(out=idx16[16 * r:16 * (r + 1), :], in_=idx16[0:16, :]))
            g.wait_ge(c_gm.sem, c_gm.n)
            g.load_library(library_config.mlp)
            emit_ffn_gpsimd_weights(g, sw1_in, sw3_in, sw2_in, 0, first_hc=2)
            c_mm.wait(g, mm_base(0) + 2 * NH_K)
            c_exp.wait(g, 16 * 8)
            for kk in range(4):
                c_xd.bump(g.dma_gather(
                    arena_b[:, 8192 * kk:8192 * (kk + 1)].rearrange("p (a b) -> p a b", b=512),
                    ag1_out.ap()[:, DPAD:DPAD + D],
                    idx16[:, 32 * kk:32 * (kk + 1)],
                    num_idxs=512, num_idxs_reg=512, elem_size=D, elem_step=ROWLEN,
                    transpose=True))
            emit_ffn_gpsimd_weights(g, we1_in, we3_in, we2_in, 1)
            c_yw[0].wait(g, 2 * YW_PER_PHASE)
            c_yw[1].wait(g, 2 * YW_PER_PHASE)
            c_cc.bump(g.collective_compute(
                "AllGather", AL.bypass, replica_groups=[list(range(NCORE))],
                ins=[ag4_in.ap().opt()], outs=[ag4_out.ap().opt()]))
            c_pg.wait(g, 3)
            c_mm.wait(g, 2 * MM_PER_PHASE)
            for b in range(NB):
                buf = b % 3
                if b >= 3:
                    c_am.wait(g, b - 2)
                c_ga[buf].bump(g.indirect_dma_start(
                    out=xg[:, bass.ts(buf, AW)], out_offset=None,
                    in_=ag1_flat,
                    in_offset=bass.IndirectOffsetOnAxis(ap=colrep[:, b:b + 1], axis=1)))
            c_mm.wait(g, mm_base(1) + 2 * NH_K)
            c_ao[0].wait(g, 16 * 64)
            c_ao[1].wait(g, 16 * 64)
            c_or.bump(g.dma_start(out=arena_b[0:128, 0:64 * 512],
                                  in_=oall.ap().rearrange("b i t -> b (i t)")))
            for rs in range(8):
                c_sv.wait(g, 64)
                c_a5.bump(g.dma_start(
                    out=ag5_in.ap().rearrange("(r i) t -> r i t", i=JD)[16 * rs:16 * (rs + 1), :, :],
                    in_=yout_b[16 * rs:16 * (rs + 1), 0:A6].rearrange("r (i t) -> r i t", t=512)))
            g.wait_ge(c_a5.sem, 16 * 8)
            c_cc.bump(g.collective_compute(
                "AllGather", AL.bypass, replica_groups=[list(range(NCORE))],
                ins=[ag5_in.ap().opt()], outs=[ag5_out.ap().opt()]))
            c_si.wait(g, 16)
            for r in range(1, 8):
                c_gm.bump(g.dma_start(out=sloti[16 * r:16 * (r + 1), :], in_=sloti[0:16, :]))
            g.wait_ge(c_gm.sem, c_gm.n)
            c_cc.wait(g, 4)
            c_sm.wait(g, 64)
            for kk in range(4):
                c_mg.bump(g.dma_gather(
                    arena_b[:, :].rearrange("p (a b) -> p a b", a=16)[:, 4 * kk:4 * (kk + 1), :],
                    ag4_out.ap()[:, :],
                    sloti[:, 32 * kk:32 * (kk + 1)],
                    num_idxs=512, num_idxs_reg=512, elem_size=D,
                    transpose=False))
            c_cc.wait(g, 5)
            for gi in range(NG):
                bi = gi % 2
                if gi >= 2:
                    c_as.wait(g, 3 * (gi - 2) + 1)
                for tq in range(4):
                    c_pat[bi].bump(g.indirect_dma_start(
                        out=att7[:, bass.ts(bi, D)][:, bass.ts(tq, 512)], out_offset=None,
                        in_=ag5_flat,
                        in_offset=bass.IndirectOffsetOnAxis(
                            ap=aidx_sb[:, 16 * tq + gi:16 * tq + gi + 1], axis=1)))
            g.wait_ge(c_out[0].sem, 16 * 8)
            g.wait_ge(c_out[1].sem, 16 * 8)

    return nc


def _consts():
    ident = np.eye(8, dtype=np.float32)
    ltri = (np.arange(128)[:, None] < np.arange(128)[None, :]).astype(np.float32)
    iota8 = np.tile(np.arange(E, dtype=np.float32)[None, :], (128, 1))
    revi = 8.0 - iota8
    iotar = np.tile(np.arange(128, dtype=np.float32)[None, :], (128, 1))
    iotap = np.arange(128, dtype=np.float32)[:, None].copy()
    return ident, ltri, iota8, revi, iotar, iotap


def kernel(L0_attn_x, L0_attn_block_values, L0_attn_cols, L0_attn_rows,
           L0_ffn_Wg, L0_ffn_W1, L0_ffn_W3, L0_ffn_W2,
           L0_ffn_sw1, L0_ffn_sw3, L0_ffn_sw2):
    x = np.ascontiguousarray(np.asarray(L0_attn_x, dtype=np.float32)).reshape(TTOK, D)
    bv = np.ascontiguousarray(np.asarray(L0_attn_block_values, dtype=np.float32))
    cols = np.ascontiguousarray(np.asarray(L0_attn_cols, dtype=np.int32).reshape(1, NB))
    rows = np.ascontiguousarray(np.asarray(L0_attn_rows, dtype=np.int32).reshape(1, NB))
    wg = np.ascontiguousarray(np.asarray(L0_ffn_Wg, dtype=np.float32))
    w1 = np.asarray(L0_ffn_W1, dtype=np.float32)
    w3 = np.asarray(L0_ffn_W3, dtype=np.float32)
    w2 = np.asarray(L0_ffn_W2, dtype=np.float32)
    sw1 = np.ascontiguousarray(np.asarray(L0_ffn_sw1, dtype=np.float32))
    sw3 = np.ascontiguousarray(np.asarray(L0_ffn_sw3, dtype=np.float32))
    sw2 = np.ascontiguousarray(np.asarray(L0_ffn_sw2, dtype=np.float32))

    wat = np.ascontiguousarray(
        bv.transpose(0, 3, 2, 1)
          .reshape(NB, NKP, 2, JD, JD)
          .transpose(0, 2, 3, 1, 4)
          .reshape(NB, 128, NKP, JD))

    ident, ltri, iota8, revi, iotar, iotap = _consts()

    nc = build_nc()
    lower_extended_insts(nc)

    in_maps = []
    p = np.arange(128)
    tau, j = p // JD, p % JD
    for core in range(NCORE):
        n = core // 4
        tq = core % 4
        pconst = ((n * 8192 + j).astype(np.uint64) * ROWLEN
                  + DPAD + tq * TQW - (KS - 1) + tau).astype(np.uint32).reshape(128, 1)
        assert ROWLEN == 4096
        gg = np.arange(16)
        aidx = np.zeros((128, 64), np.uint32)
        for tqq in range(4):
            rowbase = (n * 4 + tqq) * (NB * JD) + (core % 4) * TOKC
            aidx[:, 16 * tqq:16 * (tqq + 1)] = ((rowbase + gg[None, :] * 128 + p[:, None])
                                                * TQW).astype(np.uint32)
        cme = np.array([[core, core * TOKC, 0, 0]], dtype=np.int32)
        cc = np.arange(NG)
        tokid = (core * TOKC + cc[None, :] * 128 + p[:, None] + 1).astype(np.float32)
        locpad = (TTOK + cc[None, :] * 128 + p[:, None]).astype(np.float32)
        in_maps.append({
            "x_in": np.ascontiguousarray(x[core * TOKC:(core + 1) * TOKC]),
            "wg_in": wg, "sw1_in": sw1, "sw3_in": sw3, "sw2_in": sw2,
            "we1_in": np.ascontiguousarray(w1[core]),
            "we3_in": np.ascontiguousarray(w3[core]),
            "we2_in": np.ascontiguousarray(w2[core]),
            "wat_in": wat, "cols_in": cols, "rows_in": rows,
            "ident_in": ident, "ltri_in": ltri, "iota8_in": iota8,
            "revi_in": revi, "iotar_in": iotar, "iotap_in": iotap,
            "pconst_in": pconst, "aidx_in": aidx, "cme_in": cme,
            "zpad_in": np.zeros((1, ZW), np.float32),
            "tokid_in": tokid, "locpad_in": locpad,
        })

    res = run_bass_kernel_spmd(nc, in_maps, core_ids=list(range(NCORE)))
    out = np.concatenate([res.results[i]["y_out"] for i in range(NCORE)], axis=0)
    return out.reshape(2, 8192, 2048)
